# version 14
# baseline (speedup 1.0000x reference)
import numpy as np
import ml_dtypes
import contextlib
import concourse.bass as bass
import concourse.mybir as mybir
from concourse.bass_utils import run_bass_kernel_spmd

F32 = mybir.dt.float32
BF16 = mybir.dt.bfloat16
AF = mybir.ActivationFunctionType
ALU = mybir.AluOpType
AX = mybir.AxisListType

T, D = 4096, 1024
NH, DH, CH = 16, 64, 80
RQ, RK, RV = 6, 2, 2
NT = 32
NCH = 4
CHUNK = 1024
SUB = 8
NPAIR = 8
GRP = 8
EXP_BIAS = -20.0
EXP_SCALE = 1.0 / (RQ * RK * float(np.sqrt(DH)))

DEBUG = False
TAGS = {}


def build(debug=False):
    TAGS.clear()
    nc = bass.Bass("TRN2", target_bir_lowering=False, debug=False, dynamic_dma_scratch_size=2048)

    x_h = nc.dram_tensor("x", [T, D], F32, kind="ExternalInput").ap()
    wabqT_h = nc.dram_tensor("wabqT", [128, 8 * RQ * CH], F32, kind="ExternalInput").ap()
    wabkvT_h = nc.dram_tensor("wabkvT", [128, 8 * (RK + RV) * CH], F32, kind="ExternalInput").ap()
    woT_h = nc.dram_tensor("woT", [128, 8 * D], F32, kind="ExternalInput").ap()
    wrT_h = nc.dram_tensor("wrT", [128, 8 * D], F32, kind="ExternalInput").ap()
    wkT_h = nc.dram_tensor("wkT", [128, 8 * D], F32, kind="ExternalInput").ap()
    wvT_h = nc.dram_tensor("wvT", [128, 8 * D], F32, kind="ExternalInput").ap()
    muf_h = nc.dram_tensor("mufh", [128, 16], F32, kind="ExternalInput").ap()
    rcos_h = nc.dram_tensor("rcos", [128, NT * 32], F32, kind="ExternalInput").ap()
    rsin_h = nc.dram_tensor("rsin", [128, NT * 32], F32, kind="ExternalInput").ap()
    out_h = nc.dram_tensor("out", [T, D], F32, kind="ExternalOutput").ap()
    wbq_bf = nc.dram_tensor("wbq_bf", [128, 8 * RQ * CH], BF16, kind="Internal").ap()
    wbkv_bf = nc.dram_tensor("wbkv_bf", [128, 8 * (RK + RV) * CH], BF16, kind="Internal").ap()
    wo_bf = nc.dram_tensor("wo_bf", [128, 8 * D], BF16, kind="Internal").ap()
    wr_bf = nc.dram_tensor("wr_bf", [128, 8 * D], BF16, kind="Internal").ap()
    wk_bf = nc.dram_tensor("wk_bf", [128, 8 * D], BF16, kind="Internal").ap()
    wv_bf = nc.dram_tensor("wv_bf", [128, 8 * D], BF16, kind="Internal").ap()
    xnsp_h = nc.dram_tensor("xnsp", [128, NT * 1024], BF16,
                            kind="ExternalOutput" if debug else "Internal").ap()
    dbg_h = {}
    if debug:
        dbg_h["K"] = nc.dram_tensor("dbg_K", [128, NPAIR * T], BF16, kind="ExternalOutput").ap()
        dbg_h["V"] = nc.dram_tensor("dbg_V", [128, NT * NH * 65], BF16, kind="ExternalOutput").ap()
        dbg_h["Q"] = nc.dram_tensor("dbg_Q", [128, 8 * CHUNK], BF16, kind="ExternalOutput").ap()
        dbg_h["Y"] = nc.dram_tensor("dbg_Y", [128, 8 * CHUNK], BF16, kind="ExternalOutput").ap()

    es = contextlib.ExitStack()
    with es:
        ten = lambda *a: es.enter_context(nc.sbuf_tensor(*a))
        K_fm = ten("K_fm", [128, NPAIR * T], BF16)
        V_ext = ten("V_ext", [128, NT * NH * 65], BF16)
        wstream = ten("wstream", [128, 8 * 1024], BF16)
        yreg = ten("yreg", [128, 8 * 1024], BF16)
        qfm = ten("qfm", [128, 8 * 1024], BF16)
        ptq = ten("ptq", [128, 8 * 1024], BF16)
        xnch = ten("xnch", [128, 8 * 1025], BF16)
        ropec = ten("ropec", [128, NT * 32], BF16)
        ropes = ten("ropes", [128, NT * 32], BF16)
        lnbuf = ten("lnbuf", [128, 1024], F32)
        rrow = ten("rrow", [128, 1024], F32)
        muf = ten("muf", [128, 16], F32)
        consts = ten("consts", [128, 12], F32)
        stats = ten("stats", [128, 12], F32)
        onesb = ten("onesb", [128, 64], F32)
        bbuf = ten("bbuf", [128, 8], BF16)
        ident = ten("ident", [128, 128], BF16)

        psA = es.enter_context(nc.psum_tensor("psA", [128, 2048], F32)).ap()
        psB = es.enter_context(nc.psum_tensor("psB", [128, 2048], F32)).ap()
        psD = psB[:, 1024:2048]
        psD_bf = psB[:, 1024:1536].bitcast(BF16)

        SEMH = {}
        for s in ("DQ", "GQ", "PE", "ACT", "DVE", "POOL"):
            SEMH[s] = es.enter_context(nc.semaphore(s))
        CNT = {k: 0 for k in SEMH}
        SCHED = {k: [] for k in ("sync", "gpsimd", "vector", "scalar", "tensor")}

        def E(engine, fn, waits=(), inc=None, inc_n=1):
            wl = tuple((SEMH[s], v) for (s, v) in waits if v and v > 0)
            tgt = None
            if inc is not None:
                CNT[inc] += inc_n
                tgt = (SEMH[inc], inc_n)
                TAGS.setdefault(inc, {})[CNT[inc]] = getattr(fn, "__name__", "?")

            def run(eng, wl=wl, fn=fn, tgt=tgt):
                for h, v in wl:
                    eng.wait_ge(h, v)
                ins = fn(eng)
                if tgt is not None:
                    ins.then_inc(tgt[0], tgt[1])
            SCHED[engine].append(run)
            return CNT[inc] if inc is not None else None

        def dma(queue, sem, dst, src, waits=()):
            return E(queue, lambda e, d=dst, s=src: e.dma_start(out=d, in_=s),
                     waits, inc=sem, inc_n=16)

        CVALS = [EXP_SCALE, EXP_BIAS, 1e-5, -1.0, -0.5, 0.0, 1.0]
        for i, v in enumerate(CVALS):
            nc.const_aps.register(F32, float(v), consts[:, i:i + 1].opt())

        def Kv(pair, kt):
            return K_fm[:, pair * T + kt * 128: pair * T + kt * 128 + 128]

        def Vv(kt, head):
            off = kt * (NH * 65) + head * 65
            return V_ext[:, off: off + 65]

        def Ptv(par, hh, lo, hi):
            off = par * 1024 + hh * 512
            return ptq[:, off + lo: off + hi]

        def Pt2(par):
            return ptq[:, par * 1024:(par + 1) * 1024].rearrange("p (h w) -> p h w", h=2)

        def psA2(ping):
            return psA[:, ping * 1024:(ping + 1) * 1024].rearrange("p (h w) -> p h w", h=2)

        qp_f32 = ptq[:, 4096: 8192].bitcast(F32)
        ysc_v = qp_f32[:, 0:1024]
        comb = ptq[:, 0:8192].bitcast(F32)
        xc_v = comb[:, 0:1024]
        abqf = comb[:, 0:512]
        cscr_v = comb[:, 1024:2048]
        outst_v = comb[:, 2048:3072]

        def xr_v(cj):
            return ptq[:, cj * 1024:(cj + 1) * 1024]

        def Y_v(pair):
            return yreg[:, pair * 1024:(pair + 1) * 1024]

        def abqT_v(cj):
            return yreg[:, cj * 480:(cj + 1) * 480]

        def abkvT_v(cj):
            return wstream[:, cj * 320:(cj + 1) * 320]

        def xk_v(cj):
            return yreg[:, cj * 1024:(cj + 1) * 1024]

        p1f = yreg[:, 4096: 8192].bitcast(F32)
        xt_v = p1f[:, 0:1024]
        rs1_v = p1f[:, 1024:1152]
        rs2_v = p1f[:, 1152:1280]
        abkvf_v = p1f[:, 1280:1600]
        brot_v = p1f[:, 1600:1856]
        def xnb_b(i):
            return rrow[:, 512:1024].bitcast(BF16) if i % 2 == 0 \
                else lnbuf[:, 512:1024].bitcast(BF16)
        xnfmt_v = lnbuf[:, 0:512].bitcast(BF16)
        kact = xnch[:, 0:8192]
        xnq = xnch[:, 0:8192].rearrange("p (tile cj t) -> p tile cj t", cj=8, t=128)
        def sktm_pp(i):
            return wstream[:, 7168:8192] if i % 2 == 0 else wstream[:, 6144:7168]
        dmix_v = lnbuf[:, 0:512].bitcast(BF16)
        wsv = wstream.rearrange("p (cj n) -> p cj n", cj=8)
        xnchv = xnch.rearrange("p (cj t) -> p cj t", cj=8)

        def p0_consts(eng):
            for i, v in enumerate(CVALS):
                eng.memset(consts[:, i:i + 1], float(v))
            eng.memset(onesb[:, :], 1.0)
            eng.memset(lnbuf[:, :], 1.0)
            return eng.memset(rrow[:, :], 1.0)
        ms_consts = E("vector", p0_consts, inc="DVE")

        def p0_ident(eng):
            eng.memset(ident[:, :], 1.0)
            return eng.affine_select(out=ident[:, :], in_=ident[:, :],
                                     compare_op=ALU.is_equal, fill=0.0,
                                     base=0, pattern=[[-1, 128]], channel_multiplier=1)
        ms_ident = E("gpsimd", p0_ident, inc="POOL")

        from concourse.hw_specs import get_activation_tables
        set_id = None
        for idx, (nm, funcs) in enumerate(get_activation_tables(nc.m.arch).items()):
            fn_names = {f.name for f in funcs}
            if {"Exp", "Ln"} <= fn_names:
                set_id = idx
                break
        assert set_id is not None

        def p0_act(eng):
            eng.wait_ge(SEMH["DVE"], ms_consts)
            eng.add_instruction(mybir.InstLoadActFuncSet(
                name=nc.get_next_instruction_name(), act_func_set_id=set_id,
                ins=[], outs=[]))
            for _ in range(4):
                eng.nop(cycle_cnt=4000)
            return None
        SCHED["scalar"].append(lambda e: p0_act(e))

        dma("sync", "DQ", muf[:, :], muf_h[:, :])
        ms_mu = CNT["DQ"]

        stage_f = K_fm[:, 0:16384].bitcast(F32)
        stage_b = V_ext[:, 0:8192]
        prep_cast = 0
        prep_out = 0
        for srch, dsth, n in ((rcos_h, None, NT * 32), (rsin_h, None, NT * 32),
                              (wabqT_h, wbq_bf, 8 * RQ * CH),
                              (wabkvT_h, wbkv_bf, 8 * (RK + RV) * CH),
                              (woT_h, wo_bf, 8 * D), (wrT_h, wr_bf, 8 * D),
                              (wkT_h, wk_bf, 8 * D), (wvT_h, wv_bf, 8 * D)):
            wl = [("DVE", prep_cast)] if prep_cast else []
            pld = dma("sync", "DQ", stage_f[:, 0:n], srch[:, :], waits=wl)
            if dsth is None:
                tgt = ropec if srch is rcos_h else ropes
                prep_cast = E("vector",
                              lambda e, tgt=tgt, n=n: e.tensor_copy(tgt[:, 0:n], stage_f[:, 0:n]),
                              waits=[("DQ", pld)], inc="DVE")
            else:
                wl2 = [("DQ", pld)]
                if prep_out:
                    wl2.append(("DQ", prep_out))
                prep_cast = E("vector",
                              lambda e, n=n: e.tensor_copy(stage_b[:, 0:n], stage_f[:, 0:n]),
                              waits=wl2, inc="DVE")
                prep_out = dma("sync", "DQ", dsth[:, :], stage_b[:, 0:n],
                               waits=[("DVE", prep_cast)])
        ms_rope_g = prep_cast
        ms_prep_done = prep_out

        st8 = {
            "xt_consumed": 0,
            "xnsp_done": {},
            "psB_tr_free": 0,
            "psA_small_free": ("DVE", 0),
            "sktm_free_pe": 0,
            "skfree": {0: 0, 1: 0},
            "xnb_free": {0: 0, 1: 0},
            "p1_dve": {}, "p1_pool": {},
            "prev_exp": {},
            "prev_av": {},
            "norm_done_prev": 0,
            "pend_norm": None,
            "pend_copies": None,
            "pend_av": None,
            "yreg_Y_ok": [],
            "y_gate_lo": [],
            "y_gate_hi": [],
            "wstream_free": ("PE", 0),
            "xc_free": 0,
            "outst_dq": 0,
            "cscr_free": 0,
            "out_partial": {},
            "bbuf_saved": 0,
        }

        def _segs(off):
            return ((off, 512), (512, 1024)) if off < 512 else ((off, 1024),)

        p1st = {}
        p1L = {}

        def emit_p1_L(ti, pe_gate):
            w = [("DVE", st8["xt_consumed"]), ("PE", pe_gate)]
            ld = dma("sync", "DQ", xt_v, x_h[ti * 128:(ti + 1) * 128, :], waits=w)

            def ln_stats(eng):
                eng.reduce_sum(stats[:, 0:1], xt_v, axis=AX.X)
                eng.tensor_tensor(qp_f32[:, 0:1024], xt_v, xt_v, ALU.mult)
                return eng.reduce_sum(stats[:, 1:2], qp_f32[:, 0:1024], axis=AX.X)
            msv = E("vector", ln_stats,
                    waits=[("DQ", ld), ("DVE", ms_consts),
                           ("POOL", st8.get("kv_pool", 0))], inc="DVE")

            def ln_stats2(eng):
                eng.tensor_scalar(stats[:, 2:3], stats[:, 0:1], 1.0 / D, None, ALU.mult)
                return eng.tensor_scalar(stats[:, 3:4], stats[:, 1:2], 1.0 / D, None, ALU.mult)
            msv = E("vector", ln_stats2, waits=[("DVE", msv)], inc="DVE")

            msv = E("vector",
                    lambda e: e.scalar_tensor_tensor(stats[:, 5:6], stats[:, 2:3],
                                                     stats[:, 2:3], stats[:, 3:4],
                                                     ALU.mult, ALU.subtract),
                    waits=[("DVE", msv)], inc="DVE")

            msa = E("scalar",
                    lambda e: e.activation(stats[:, 6:7], stats[:, 5:6], AF.Ln,
                                           bias=1e-5, scale=-1.0),
                    waits=[("DVE", msv)], inc="ACT")
            msa = E("scalar",
                    lambda e: e.activation(stats[:, 7:8], stats[:, 6:7], AF.Exp,
                                           bias=0.0, scale=-0.5),
                    waits=[("ACT", msa)], inc="ACT")

            msn = E("vector",
                    lambda e: e.scalar_tensor_tensor(stats[:, 9:10], stats[:, 2:3], -1.0,
                                                     stats[:, 7:8], ALU.mult, ALU.mult),
                    waits=[("ACT", msa)], inc="DVE")
            ms_xnb = E("vector",
                       lambda e, d=xnb_b(ti): e.tensor_scalar(d, xt_v, stats[:, 7:8],
                                                              stats[:, 9:10],
                                                              ALU.mult, ALU.add),
                       waits=[("DVE", msn), ("PE", st8["xnb_free"][ti % 2])], inc="DVE")
            st8["xt_consumed"] = ms_xnb
            p1L[ti] = ms_xnb

        def emit_p1_M(ti, abkv_ld):
            ms_xnb = p1L.pop(ti)

            def xpose(eng):
                ins = None
                for cj in range(8):
                    ins = eng.transpose(psD_bf[:, cj * 128:(cj + 1) * 128],
                                        xnb_b(ti)[:, cj * 128:(cj + 1) * 128], ident[:, :])
                return ins
            ms_tp = E("tensor", xpose,
                      waits=[("DVE", ms_xnb), ("POOL", ms_ident),
                             ("DVE", st8["psB_tr_free"])], inc="PE")
            st8["xnb_free"][ti % 2] = ms_tp

            wv = [("PE", ms_tp)]
            if ti > 0 and (ti - 1) in st8["xnsp_done"]:
                wv.append(("DQ", st8["xnsp_done"][ti - 1]))
            ms_xnfmt = E("vector", lambda e: e.tensor_copy(xnfmt_v, psD_bf[:, 0:1024]),
                         waits=wv, inc="DVE")
            st8["psB_tr_free"] = ms_xnfmt

            sp = dma("sync", "DQ", xnsp_h[:, ti * 1024:(ti + 1) * 1024],
                     xnfmt_v, waits=[("DVE", ms_xnfmt)])
            st8["xnsp_done"][ti] = sp

            def abkv_mm(eng):
                ins = None
                for cj in range(8):
                    ins = eng.matmul(psD[:, 0:320], xnfmt_v[:, cj * 128:(cj + 1) * 128],
                                     abkvT_v(cj), start=(cj == 0), stop=(cj == 7))
                return ins
            ms_mm = E("tensor", abkv_mm,
                      waits=[("DVE", ms_xnfmt), ("DQ", abkv_ld),
                             st8["psA_small_free"]], inc="PE")
            ms_ev = E("vector", lambda e: e.tensor_copy(abkvf_v, psD[:, 0:320]),
                      waits=[("PE", ms_mm), ("POOL", st8.get("kv_pool", 0))], inc="DVE")
            st8["psA_small_free"] = ("DVE", ms_ev)

            cosv = ropec[:, ti * 32:(ti + 1) * 32][:, None, :].broadcast_to([128, 4, 32])
            sinv = ropes[:, ti * 32:(ti + 1) * 32][:, None, :].broadcast_to([128, 4, 32])
            abkv_j = abkvf_v.rearrange("p (j c) -> p j c", j=4)
            x1v, x2v = abkv_j[:, :, 16:48], abkv_j[:, :, 48:80]
            brot_j = brot_v.rearrange("p (j c) -> p j c", j=4)
            rs1j = rs1_v.rearrange("p (j c) -> p j c", j=4)
            rs2j = rs2_v.rearrange("p (j c) -> p j c", j=4)

            ms_rope = E("vector",
                        lambda e: (e.tensor_tensor(rs1_v, x1v, cosv, ALU.mult),
                                   e.tensor_tensor(rs2_v, x2v, sinv, ALU.mult))[-1],
                        waits=[("DVE", ms_ev), ("DVE", ms_rope_g)], inc="DVE")
            ms_rope = E("vector",
                        lambda e: e.tensor_tensor(brot_j[:, :, 0:32], rs1j, rs2j, ALU.add),
                        waits=[("DVE", ms_rope)], inc="DVE")
            ms_rope = E("vector",
                        lambda e: (e.tensor_tensor(rs1_v, x1v, sinv, ALU.mult),
                                   e.tensor_tensor(rs2_v, x2v, cosv, ALU.mult))[-1],
                        waits=[("DVE", ms_rope)], inc="DVE")
            ms_rope = E("vector",
                        lambda e: e.tensor_tensor(brot_j[:, :, 32:64], rs2j, rs1j, ALU.subtract),
                        waits=[("DVE", ms_rope)], inc="DVE")

            pBv = qp_f32[:, 1024:2048].rearrange("p (h c) -> p h c", h=NH)

            def abj(j):
                return abkvf_v[:, j * 80: j * 80 + 16][:, :, None].broadcast_to([128, NH, DH])

            def brj(j):
                return brot_v[:, j * 64:(j + 1) * 64][:, None, :].broadcast_to([128, NH, DH])

            sktm = sktm_pp(ti)
            sktm3 = sktm[:, 0:1024].rearrange("p (h c) -> p h c", h=NH)

            def kv_s1(eng, sktm3=sktm3):
                eng.tensor_tensor(sktm3, abj(0), brj(0), ALU.mult)
                return eng.tensor_tensor(pBv, abj(1), brj(1), ALU.mult)
            ms_kv = E("gpsimd", kv_s1,
                      waits=[("DVE", ms_rope), ("PE", st8["skfree"][ti % 2])], inc="POOL")
            ms_kv = E("gpsimd",
                      lambda e, sktm=sktm: e.tensor_tensor(
                          sktm[:, 0:1024], sktm[:, 0:1024], qp_f32[:, 1024:2048], ALU.add),
                      waits=[("POOL", ms_kv)], inc="POOL")
            off_v = ti * (NH * 65)
            vdst = V_ext[:, off_v: off_v + NH * 65].rearrange("p (h c) -> p h c", c=65)

            def kv_s3(eng, vdst=vdst):
                eng.tensor_tensor(vdst[:, :, 0:64], abj(2), brj(2), ALU.mult)
                return eng.tensor_tensor(pBv, abj(3), brj(3), ALU.mult)
            ms_kv = E("gpsimd", kv_s3, waits=[("POOL", ms_kv)], inc="POOL")

            def v_fin(eng, vdst=vdst):
                eng.tensor_tensor(vdst[:, :, 0:64], vdst[:, :, 0:64], pBv, ALU.add)
                return eng.memset(vdst[:, :, 64:65], 2.0)
            ms_kv = E("gpsimd", v_fin, waits=[("POOL", ms_kv)], inc="POOL")
            st8["kv_pool"] = ms_kv
            p1st[ti] = ms_kv

        def emit_p1_B(ti):
            sktm = sktm_pp(ti)
            ms_kv = p1st.pop(ti)

            def ktp(eng, sktm=sktm):
                ins = None
                for pr in range(8):
                    ins = eng.transpose(psD_bf[:, pr * 128:(pr + 1) * 128],
                                        sktm[:, pr * 128:(pr + 1) * 128], ident[:, :])
                return ins
            ms_ktp = E("tensor", ktp,
                       waits=[("POOL", ms_kv), ("DVE", st8["psB_tr_free"]),
                              st8["psA_small_free"]], inc="PE")
            st8["sktm_free_pe"] = ms_ktp
            st8["skfree"][ti % 2] = ms_ktp
            Kdst = K_fm.rearrange("p (pr t) -> p pr t", pr=NPAIR)[:, :, ti * 128:(ti + 1) * 128]
            ms_kf = E("vector",
                      lambda e, Kdst=Kdst: e.tensor_copy(
                          Kdst, psD_bf[:, 0:1024].rearrange("p (pr t) -> p pr t", pr=8)),
                      waits=[("PE", ms_ktp)], inc="DVE")
            st8["psB_tr_free"] = ms_kf

        def emit_p1_group(g, yreg_free, deferred=False):
            abkv_ld = dma("sync", "DQ", wstream[:, 0:2560], wbkv_bf[:, :],
                          waits=list(yreg_free) + [("DQ", ms_prep_done),
                                                   ("PE", CNT["PE"])])
            pe_gate = CNT["PE"]

            def fin():
                st8["p1_dve"][g] = CNT["DVE"]
                st8["p1_pool"][g] = CNT["POOL"]
                st8["y_gate_hi"] = [("DVE", CNT["DVE"]), ("POOL", CNT["POOL"])]
                st8["p1_pe_cnt"] = CNT["PE"]
            t0 = g * GRP

            def elem(bi=None, mi=None, li=None):
                def go():
                    if bi is not None:
                        emit_p1_B(t0 + bi)
                    if mi is not None:
                        emit_p1_M(t0 + mi, abkv_ld)
                    if li is not None:
                        emit_p1_L(t0 + li, pe_gate)
                return go
            tiles = [elem(li=0), elem(mi=0, li=1), elem(mi=1, li=2)]
            for j in range(5):
                tiles.append(elem(bi=j, mi=j + 2, li=j + 3))
            tiles += [elem(bi=5, mi=7), elem(bi=6), elem(bi=7), fin]
            if not deferred:
                for t in tiles:
                    t()
                return []
            return tiles

        def emit_qbuild(c, yreg_free, ptq_free, qfm_free, xnch_free):
            q0 = c * CHUNK
            w = list(xnch_free) + [("DQ", st8["xnsp_done"][(c + 1) * GRP - 1])]
            ld = dma("sync", "DQ", xnch[:, 0:8192],
                     xnsp_h[:, 8 * c * 1024: (8 * c + 8) * 1024], waits=w)
            if c == 0:
                msz = E("vector", lambda e: e.memset(bbuf[:, :], 0.0),
                        waits=list(xnch_free), inc="DVE")
                xnch_ready = [("DQ", ld), ("DVE", msz)]
            else:
                xnch_ready = [("DQ", ld), ("DVE", st8["bbuf_saved"])]

            pe_at_qb = CNT["PE"]
            abq_ld = dma("sync", "DQ", yreg[:, 0:3840], wbq_bf[:, :],
                         waits=list(yreg_free) + [("DQ", ms_prep_done)])

            ms_q = 0
            ev_d, tp_d, jn_d, pl_d, qv_d = {}, {}, {}, {}, {}
            qc_prev = [0]
            pacc = cscr_v
            pscr = xt_v

            sktm_b = sktm_pp

            def abp(r, which):
                if which == 0:
                    return abqf[:, r * 80: r * 80 + 16][:, :, None].broadcast_to([128, NH, DH])
                return abqf[:, r * 80 + 16: r * 80 + 80][:, None, :].broadcast_to([128, NH, DH])

            def emit_join(sub):
                wj = [("POOL", pl_d[sub])]
                if sub - 2 in tp_d:
                    wj.append(("PE", tp_d[sub - 2]))
                jn_d[sub] = E("vector",
                              lambda e, sub=sub: e.tensor_tensor(
                                  sktm_b(sub)[:, 0:1024], qp_f32[:, 0:1024],
                                  pacc, ALU.add),
                              waits=wj, inc="DVE")

            def emit_qtp(sub):
                sk = sktm_b(sub)

                def qtp(eng, sk=sk):
                    ins = None
                    for pr in range(8):
                        ins = eng.transpose(psD_bf[:, pr * 128:(pr + 1) * 128],
                                            sk[:, pr * 128:(pr + 1) * 128], ident[:, :])
                    return ins
                wt = [("DVE", jn_d[sub]), ("DVE", st8["psB_tr_free"])]
                if sub + 1 in ev_d:
                    wt.append(("ACT", ev_d[sub + 1]))
                ms_qtp = E("tensor", qtp, waits=wt, inc="PE")
                tp_d[sub] = ms_qtp
                st8["sktm_free_pe"] = ms_qtp
                st8["skfree"][sub % 2] = ms_qtp
                Qdst = qfm.rearrange("p (pr t) -> p pr t", pr=NPAIR)[:, :, sub * 128:(sub + 1) * 128]
                wq2 = [("PE", ms_qtp)]
                if sub == 0:
                    wq2 += list(qfm_free)
                msq = E("vector",
                        lambda e, Qdst=Qdst: e.tensor_copy(
                            Qdst, psD_bf[:, 0:1024].rearrange("p (pr t) -> p pr t", pr=8)),
                        waits=wq2, inc="DVE")
                qv_d[sub] = msq
                st8["psB_tr_free"] = msq
                return msq

            for sub in range(SUB):
                def abq_mm(eng, sub=sub):
                    ins = None
                    for cj in range(8):
                        lhsT = xnch[:, sub * 1024 + cj * 128: sub * 1024 + cj * 128 + 128]
                        ins = eng.matmul(psD[:, 0:480], lhsT, abqT_v(cj),
                                         start=(cj == 0), stop=(cj == 7))
                    return ins
                ms_mm = E("tensor", abq_mm,
                          waits=xnch_ready + [("DQ", abq_ld), st8["psA_small_free"],
                                              ("DVE", st8["psB_tr_free"])],
                          inc="PE")
                wv = [("PE", ms_mm)]
                if sub == 0:
                    wv += list(ptq_free)
                else:
                    wv += [("DVE", qc_prev[0]), ("POOL", pl_d[sub - 1])]
                ms_ev = E("scalar",
                          lambda e: e.activation(abqf[:, 0:480], psD[:, 0:480],
                                                 AF.Identity, bias=0.0, scale=1.0),
                          waits=wv, inc="ACT")
                st8["psA_small_free"] = ("ACT", ms_ev)
                ev_d[sub] = ms_ev

                if sub >= 1:
                    emit_join(sub - 1)

                def qcon_pool(eng, sub=sub):
                    pacc3 = pacc.rearrange("p (h c) -> p h c", h=NH)
                    pscr3 = pscr.rearrange("p (h c) -> p h c", h=NH)
                    eng.tensor_tensor(pacc3, abp(4, 0), abp(4, 1), ALU.mult)
                    eng.tensor_tensor(pscr3, abp(5, 0), abp(5, 1), ALU.mult)
                    return eng.tensor_tensor(pacc, pacc, pscr, ALU.add)
                wp = [("ACT", ms_ev)]
                if sub == 0:
                    wp += [("PE", pe_at_qb), ("DVE", st8["cscr_free"]),
                           ("DVE", st8["xt_consumed"]), ("DQ", st8["outst_dq"])]
                else:
                    wp.append(("DVE", jn_d[sub - 1]))
                pl_d[sub] = E("gpsimd", qcon_pool, waits=wp, inc="POOL")

                def qcon(eng, sub=sub):
                    qacc = qp_f32[:, 0:1024].rearrange("p (h c) -> p h c", h=NH)
                    qscr = qp_f32[:, 1024:2048].rearrange("p (h c) -> p h c", h=NH)
                    ins = eng.tensor_tensor(qacc, abp(0, 0), abp(0, 1), ALU.mult)
                    for r in (1, 2, 3):
                        eng.tensor_tensor(qscr, abp(r, 0), abp(r, 1), ALU.mult)
                        ins = eng.tensor_tensor(qacc, qacc, qscr, ALU.add)
                    return ins
                wq = [("ACT", ms_ev)]
                if sub == 0:
                    wq.append(("PE", pe_at_qb))
                    wq.append(("POOL", CNT["POOL"] - 1))
                qc_prev[0] = E("vector", qcon, waits=wq, inc="DVE")

                if sub >= 1:
                    ms_q = emit_qtp(sub - 1)
            emit_join(SUB - 1)
            ms_q = emit_qtp(SUB - 1)
            return ms_q, xnch_ready

        def make_norm(s, pair, av_cell):
            half = s % 2
            cell_nc = [0]

            def docopies():
                av_last = av_cell[0]
                def norm_copies(eng):
                    eng.tensor_copy(ysc_v[0:64, 0:512], psB[0:64, 0:512])
                    eng.tensor_copy(ysc_v[64:128, 0:512], psB[0:64, 512:1024])
                    eng.tensor_copy(lnbuf[0:1, 0:512], psB[64:65, 0:512])
                    return eng.tensor_copy(lnbuf[32:33, 0:512], psB[64:65, 512:1024])
                ms_nc = E("vector", norm_copies,
                          waits=[("PE", av_last), ("DQ", st8["outst_dq"])], inc="DVE")
                cell_nc[0] = ms_nc
                st8["psB_av_free"] = ms_nc

            def donorm():
                ms_nc = cell_nc[0]
                ms_le = E("scalar",
                          lambda e: e.activation(rrow[0:33, 0:512], lnbuf[0:33, 0:512],
                                                 AF.Ln, bias=0.0, scale=1.0),
                          waits=[("DVE", ms_nc)], inc="ACT")
                ms_le = E("scalar",
                          lambda e: e.activation(rrow[0:33, 0:512], rrow[0:33, 0:512],
                                                 AF.Exp, bias=0.0, scale=-1.0),
                          waits=[("ACT", ms_le)], inc="ACT")

                def norm_rs(eng):
                    ins = None
                    for hh, rbase in ((0, 0), (1, 32)):
                        ins = eng.matmul(psD[hh * 64:(hh + 1) * 64, 512:1024],
                                         onesb[rbase:rbase + 1, :],
                                         rrow[rbase:rbase + 1, 0:512],
                                         start=True, stop=True,
                                         tile_position=(rbase, hh * 64))
                    return ins
                ms_rs = E("tensor", norm_rs,
                          waits=[("ACT", ms_le),
                                 ("DVE", st8.get("psD_norm_free", 0))], inc="PE")

                gate = st8["y_gate_lo"] if pair < 4 else st8["y_gate_hi"]
                ms_nm = E("vector",
                          lambda e: e.tensor_tensor(
                              Y_v(pair)[:, half * 512:(half + 1) * 512],
                              ysc_v[:, 0:512], psD[:, 512:1024], ALU.mult),
                          waits=[("PE", ms_rs)] + list(gate), inc="DVE")
                st8["norm_done_prev"] = ms_nm
                st8["psD_norm_free"] = ms_nm
                return ms_nm
            return docopies, donorm

        def emit_subchunk(s, ms_q, p1_tiles):
            q0 = 512 * s
            NKT = 4 * (s + 1)
            qv = qfm.rearrange("p (pr t) -> p pr t", pr=NPAIR)
            half = s % 2

            for pair in range(NPAIR):
                hA, hB = 2 * pair, 2 * pair + 1
                exp_ms, av_ms, mask_ms = {}, {}, {}

                def emit_S(kt):
                    off = max(0, kt * 128 - q0)
                    ping = kt % 2

                    def smm(eng, kt=kt, off=off, pair=pair, ping=ping):
                        ins = None
                        for hh, base in ((0, 0), (1, 64)):
                            lhsT = Kv(pair, kt)[base:base + 64, :]
                            ins = eng.matmul(
                                psA[:, ping * 1024 + hh * 512 + off: ping * 1024 + hh * 512 + 512],
                                lhsT,
                                qv[base:base + 64, pair, half * 512 + off: half * 512 + 512],
                                start=True, stop=True, tile_position=(base, 0))
                        return ins
                    w = [("DVE", ms_q), ("DVE", st8["p1_dve"][kt // 8]),
                         ("POOL", st8["p1_pool"][kt // 8])]
                    if kt >= 2:
                        w.append(("ACT", exp_ms[kt - 2]))
                    else:
                        pk = st8["prev_exp"].get(kt)
                        if pk:
                            w.append(("ACT", pk))
                    if kt == 0:
                        w.append(st8["psA_small_free"])
                    return E("tensor", smm, waits=w, inc="PE")

                def emit_exp(kt):
                    off = max(0, kt * 128 - q0)
                    ping = kt % 2
                    par = kt % 4

                    def expf(eng, off=off, ping=ping, par=par):
                        return eng.activation(Pt2(par)[:, :, off:512],
                                              psA2(ping)[:, :, off:512],
                                              AF.Exp, bias=EXP_BIAS, scale=EXP_SCALE)
                    we = [("PE", s_ms[kt])]
                    if kt >= 4:
                        we.append(("PE", av_ms[kt - 4]))
                    else:
                        pk = st8["prev_av"].get(kt)
                        if pk:
                            we.append(("PE", pk))
                    exp_ms[kt] = E("scalar", expf, waits=we, inc="ACT")
                    if kt * 128 >= q0:
                        w_cols = 512 - off

                        def maskf(eng, off=off, par=par, w_cols=w_cols):
                            return eng.affine_select(
                                out=Pt2(par)[:, :, off:512], in_=Pt2(par)[:, :, off:512],
                                compare_op=ALU.is_ge, fill=0.0,
                                base=0, pattern=[[0, 2], [1, w_cols]], channel_multiplier=-1)
                        mask_ms[kt] = E("gpsimd", maskf, waits=[("ACT", exp_ms[kt])], inc="POOL")

                def emit_AV(kt, exp_ms=exp_ms, av_ms=av_ms, mask_ms=mask_ms, hA=hA, hB=hB, NKT=NKT, q0=q0):
                    off = max(0, kt * 128 - q0)
                    ping = kt % 2

                    par = kt % 4

                    def avf(eng, kt=kt, off=off, par=par, NKT=NKT, hA=hA, hB=hB):
                        ins = None
                        for hh, head in ((0, hA), (1, hB)):
                            ins = eng.matmul(psB[0:65, hh * 512 + off: hh * 512 + 512],
                                             Vv(kt, head), Ptv(par, hh, off, 512),
                                             start=(kt == 0), stop=(kt == NKT - 1))
                        return ins
                    wa = [("ACT", exp_ms[kt])]
                    if kt in mask_ms:
                        wa.append(("POOL", mask_ms[kt]))
                    if kt == 0:
                        wa.append(("DVE", st8.get("psB_av_free", 0)))
                    av_ms[kt] = E("tensor", avf, waits=wa, inc="PE")

                s_ms = {}
                norm_kt = 3 if NKT <= 4 else (5 if NKT <= 8 else 7)
                for kt in range(NKT):
                    s_ms[kt] = emit_S(kt)
                    emit_exp(kt)
                    if kt == 0 and st8["pend_av"] is not None:
                        st8["pend_av"]()
                        st8["pend_av"] = None
                    if kt == 1 and st8["pend_copies"] is not None:
                        st8["pend_copies"]()
                        st8["pend_copies"] = None
                    if kt == norm_kt and st8["pend_norm"] is not None:
                        st8["pend_norm"]()
                        st8["pend_norm"] = None
                    if kt >= 1:
                        emit_AV(kt - 1)
                av_cell = [None]

                def pend_av(emit_AV=emit_AV, av_ms=av_ms, av_cell=av_cell, NKT=NKT):
                    emit_AV(NKT - 1)
                    av_cell[0] = av_ms[NKT - 1]
                    st8["prev_av"] = {(kk - (NKT - 4)): av_ms[kk]
                                      for kk in range(max(0, NKT - 4), NKT)}
                st8["prev_exp"] = {0: exp_ms[NKT - 2], 1: exp_ms[NKT - 1]}
                st8["prev_av"] = {(kk - (NKT - 4)): av_ms[kk]
                                  for kk in range(max(0, NKT - 4), NKT - 1)}
                st8["pend_av"] = pend_av
                st8["pend_copies"], st8["pend_norm"] = make_norm(s, pair, av_cell)

                for _ in range(2):
                    if p1_tiles:
                        p1_tiles.pop(0)()
                        if p1_tiles and not callable(p1_tiles[0]):
                            p1_tiles.pop(0)()

        def emit_tail(c, xnch_ready, norm_all):
            q0 = c * CHUNK
            yv = yreg.rearrange("p (pr t) -> p pr t", pr=NPAIR)

            wo_ld0 = dma("sync", "DQ", wstream[:, 0:7168], wo_bf[:, 0:7168],
                         waits=[("PE", st8["p1_pe_cnt"]), st8["wstream_free"]])
            wo_ld = dma("sync", "DQ", wstream[:, 7168:8192], wo_bf[:, 7168:8192],
                        waits=[("PE", CNT["PE"])])
            wo_evac = {}
            for sub in range(SUB):
                wx = [("DVE", st8["xc_free"])] if st8["xc_free"] else [("DVE", norm_all)]
                xl = dma("sync", "DQ", xc_v, x_h[q0 + sub * 128: q0 + sub * 128 + 128, :], waits=wx)
                ms_p1 = 0
                for hf in range(2):
                    pc = sub * 2 + hf

                    def wo_mm(eng, sub=sub, hf=hf):
                        ins = None
                        for pr in range(8):
                            lhsT = yv[:, pr, sub * 128:(sub + 1) * 128]
                            ins = eng.matmul(psD[:, hf * 512:(hf + 1) * 512], lhsT,
                                             wsv[:, pr, hf * 512:(hf + 1) * 512],
                                             start=(pr == 0), stop=(pr == 7))
                        return ins
                    w = [("DQ", wo_ld), ("DVE", norm_all)]
                    if pc == 0:
                        w.append(st8["psA_small_free"])
                        w.append(("DVE", st8["psB_tr_free"]))
                    if pc >= 2:
                        w.append(("DVE", wo_evac[pc - 2]))
                    ms_wo = E("tensor", wo_mm, waits=w, inc="PE")

                    def part1(eng, hf=hf):
                        return eng.tensor_tensor(outst_v[:, hf * 512:(hf + 1) * 512],
                                                 psD[:, hf * 512:(hf + 1) * 512],
                                                 xc_v[:, hf * 512:(hf + 1) * 512], ALU.add)
                    wp = [("PE", ms_wo), ("DQ", xl)]
                    if st8["outst_dq"]:
                        wp.append(("DQ", st8["outst_dq"]))
                    ms_p1 = E("vector", part1, waits=wp, inc="DVE")
                    wo_evac[pc] = ms_p1
                st8["xc_free"] = ms_p1
                stq = dma("sync", "DQ", out_h[q0 + sub * 128: q0 + sub * 128 + 128, :],
                          outst_v, waits=[("DVE", ms_p1)])
                st8["outst_dq"] = stq
                st8["out_partial"][(c, sub)] = stq
            pe_after_wo = CNT["PE"]
            st8["wstream_free"] = ("PE", pe_after_wo)

            wr_ld = dma("sync", "DQ", wstream[:, :], wr_bf[:, :],
                        waits=[st8["wstream_free"]])

            last_st = st8["out_partial"][(c, SUB - 1)]
            dmix3 = dmix_v.rearrange("p (tile t) -> p tile t", t=128)
            mix_w = [("DQ", last_st), ("DQ", ms_mu),
                     ("PE", pe_after_wo), ("PE", st8["sktm_free_pe"])] + list(xnch_ready)
            ms_mix = 0
            for cj in range(8):
                def mix_d(eng, cj=cj):
                    eng.tensor_tensor(dmix3[:, :, 1:128], xnq[:, :, cj, 1:128],
                                      xnq[:, :, cj, 0:127], ALU.subtract)
                    eng.tensor_tensor(dmix3[:, 1:8, 0:1], xnq[:, 1:8, cj, 0:1],
                                      xnq[:, 0:7, cj, 127:128], ALU.subtract)
                    return eng.tensor_tensor(dmix3[:, 0:1, 0:1], xnq[:, 0:1, cj, 0:1],
                                             bbuf[:, cj:cj + 1], ALU.subtract)
                wmd = list(mix_w) if cj == 0 else [("DVE", ms_mix)]
                ms_d = E("vector", mix_d, waits=wmd, inc="DVE")

                def mix_rk(eng, cj=cj):
                    xr3 = ptq[:, cj * 1024:(cj + 1) * 1024].rearrange("p (tile t) -> p tile t", t=128)
                    xk3 = yreg[:, cj * 1024:(cj + 1) * 1024].rearrange("p (tile t) -> p tile t", t=128)
                    ins = None
                    for dst3, mcol in ((xr3, muf[:, cj:cj + 1]), (xk3, muf[:, 8 + cj:9 + cj])):
                        eng.scalar_tensor_tensor(dst3[:, :, 1:128], dmix3[:, :, 1:128], mcol,
                                                 xnq[:, :, cj, 0:127], ALU.mult, ALU.add)
                        eng.scalar_tensor_tensor(dst3[:, 1:8, 0:1], dmix3[:, 1:8, 0:1], mcol,
                                                 xnq[:, 0:7, cj, 127:128], ALU.mult, ALU.add)
                        ins = eng.scalar_tensor_tensor(dst3[:, 0:1, 0:1], dmix3[:, 0:1, 0:1], mcol,
                                                      bbuf[:, cj:cj + 1], ALU.mult, ALU.add)
                    return ins
                ms_mix = E("vector", mix_rk, waits=[("DVE", ms_d)], inc="DVE")
            ms_bb = E("vector",
                      lambda e: e.tensor_copy(bbuf[:, :], xnq[:, 7, :, 127]),
                      waits=[("DVE", ms_mix)], inc="DVE")
            st8["bbuf_saved"] = ms_bb

            r_done = {}
            for sub in range(SUB):
                for hf in range(2):
                    pc = sub * 2 + hf

                    def r_mm(eng, sub=sub, hf=hf):
                        ins = None
                        for cj in range(8):
                            lhsT = ptq[:, cj * 1024 + sub * 128: cj * 1024 + sub * 128 + 128]
                            ins = eng.matmul(psD[:, hf * 512:(hf + 1) * 512], lhsT,
                                             wsv[:, cj, hf * 512:(hf + 1) * 512],
                                             start=(cj == 0), stop=(cj == 7))
                        return ins
                    w = [("DQ", wr_ld), ("DVE", ms_mix)]
                    if pc >= 2:
                        w.append(("ACT", r_done[pc - 2]))
                    ms_rmm = E("tensor", r_mm, waits=w, inc="PE")

                    lnh = lnbuf[:, hf * 512:(hf + 1) * 512]
                    ms_s1 = E("scalar",
                              lambda e, hf=hf, lnh=lnh: e.activation(
                                  lnh, psD[:, hf * 512:(hf + 1) * 512],
                                  AF.Exp, bias=0.0, scale=-1.0),
                              waits=[("PE", ms_rmm)], inc="ACT")
                    ms_s2 = E("scalar",
                              lambda e, lnh=lnh: e.activation(lnh, lnh, AF.Ln,
                                                              bias=1.0, scale=1.0),
                              waits=[("ACT", ms_s1)], inc="ACT")
                    r_done[pc] = E("scalar",
                                   lambda e, sub=sub, hf=hf, lnh=lnh: e.activation(
                                       qfm[:, sub * 1024 + hf * 512: sub * 1024 + hf * 512 + 512],
                                       lnh, AF.Exp, bias=0.0, scale=-1.0),
                                   waits=[("ACT", ms_s2)], inc="ACT")
            ms_r_all = r_done[2 * SUB - 1]
            pe_after_r = CNT["PE"]
            st8["wstream_free"] = ("PE", pe_after_r)

            wk_ld = dma("sync", "DQ", wstream[:, :], wk_bf[:, :],
                        waits=[st8["wstream_free"]])
            k_done = {}
            for dblk in range(8):
                for hf in range(2):
                    pc = dblk * 2 + hf

                    def k_mm(eng, dblk=dblk, hf=hf):
                        ins = None
                        for cj in range(8):
                            lhsT = wsv[:, cj, dblk * 128:(dblk + 1) * 128]
                            ins = eng.matmul(psD[:, hf * 512:(hf + 1) * 512], lhsT,
                                             yreg[:, cj * 1024 + hf * 512: cj * 1024 + hf * 512 + 512],
                                             start=(cj == 0), stop=(cj == 7))
                        return ins
                    w = [("DQ", wk_ld), ("DVE", ms_mix)]
                    if pc >= 2:
                        w.append(("DVE", k_done[pc - 2]))
                    ms_kmm = E("tensor", k_mm, waits=w, inc="PE")

                    def krelu(eng, hf=hf):
                        return eng.activation(cscr_v[:, hf * 512:(hf + 1) * 512],
                                              psD[:, hf * 512:(hf + 1) * 512],
                                              AF.Relu, bias=0.0, scale=1.0)
                    wka = [("PE", ms_kmm), ("PE", pe_after_r)]
                    if st8["cscr_free"]:
                        wka.append(("DVE", st8["cscr_free"]))
                    ms_kr = E("scalar", krelu, waits=wka, inc="ACT")

                    def ksq(eng, dblk=dblk, hf=hf):
                        cs = cscr_v[:, hf * 512:(hf + 1) * 512]
                        return eng.tensor_tensor(
                            kact[:, dblk * 1024 + hf * 512: dblk * 1024 + hf * 512 + 512],
                            cs, cs, ALU.mult)
                    k_done[pc] = E("vector", ksq, waits=[("ACT", ms_kr)], inc="DVE")
                    st8["cscr_free"] = k_done[pc]
            ms_k_all = k_done[15]
            pe_after_k = CNT["PE"]
            st8["wstream_free"] = ("PE", pe_after_k)

            wv_ld = dma("sync", "DQ", wstream[:, :], wv_bf[:, :],
                        waits=[st8["wstream_free"]])
            fin_done = {}
            for sub in range(SUB):
                ol = dma("sync", "DQ", xc_v, out_h[q0 + sub * 128: q0 + sub * 128 + 128, :],
                         waits=[("DQ", st8["out_partial"][(c, sub)]),
                                ("DVE", st8["xc_free"]), ("PE", pe_after_r)])
                ms_f = 0
                for hf in range(2):
                    pc = sub * 2 + hf

                    def v_mm(eng, sub=sub, hf=hf):
                        ins = None
                        for dblk in range(8):
                            lhsT = kact[:, dblk * 1024 + sub * 128: dblk * 1024 + sub * 128 + 128]
                            ins = eng.matmul(psD[:, hf * 512:(hf + 1) * 512], lhsT,
                                             wsv[:, dblk, hf * 512:(hf + 1) * 512],
                                             start=(dblk == 0), stop=(dblk == 7))
                        return ins
                    w = [("DQ", wv_ld), ("DVE", ms_k_all), ("ACT", ms_r_all)]
                    if pc >= 2:
                        w.append(("DVE", fin_done[pc - 2]))
                    ms_vmm = E("tensor", v_mm, waits=w, inc="PE")

                    def fin(eng, sub=sub, hf=hf):
                        cs = cscr_v[:, hf * 512:(hf + 1) * 512]
                        eng.tensor_tensor(cs, qfm[:, sub * 1024 + hf * 512: sub * 1024 + hf * 512 + 512],
                                          psD[:, hf * 512:(hf + 1) * 512], ALU.mult)
                        return eng.tensor_tensor(outst_v[:, hf * 512:(hf + 1) * 512],
                                                 cs, xc_v[:, hf * 512:(hf + 1) * 512], ALU.add)
                    wf = [("PE", ms_vmm), ("DQ", ol), ("DVE", st8["cscr_free"]),
                          ("DQ", st8["outst_dq"])]
                    ms_f = E("vector", fin, waits=wf, inc="DVE")
                    fin_done[pc] = ms_f
                    st8["cscr_free"] = ms_f
                st8["xc_free"] = ms_f
                stq = dma("sync", "DQ", out_h[q0 + sub * 128: q0 + sub * 128 + 128, :],
                          outst_v, waits=[("DVE", ms_f)])
                st8["outst_dq"] = stq
            st8["wstream_free"] = ("PE", CNT["PE"])
            st8["psA_small_free"] = ("DVE", fin_done[2 * SUB - 1])
            st8["psD_norm_free"] = fin_done[2 * SUB - 1]
            st8["psB_tr_free"] = ms_k_all
            return fin_done[SUB - 1]

        emit_p1_group(0, [("PE", 0)])
        last_fin = 0
        for c in range(NCH):
            if c == 0:
                yreg_free = [("POOL", 0)]
                ptq_free = [("DVE", 0)]
                qfm_free = [("DVE", 0)]
                xnch_free = [("DVE", 0)]
            else:
                yreg_free = [("DVE", last_fin), ("PE", CNT["PE"])]
                ptq_free = [("DVE", last_fin)]
                qfm_free = [("DVE", last_fin)]
                xnch_free = [("PE", CNT["PE"])]

            ms_q, xnch_ready = emit_qbuild(c, yreg_free, ptq_free, qfm_free, xnch_free)
            st8["y_gate_lo"] = [("PE", CNT["PE"])]

            if c + 1 <= 3:
                p1_tiles = emit_p1_group(c + 1, [("PE", CNT["PE"])], deferred=True)
            else:
                p1_tiles = []
                st8["p1_pe_cnt"] = CNT["PE"]

            emit_subchunk(2 * c, ms_q, p1_tiles)
            while p1_tiles:
                p1_tiles.pop(0)()
            emit_subchunk(2 * c + 1, ms_q, [])
            st8["pend_av"]()
            st8["pend_av"] = None
            st8["pend_copies"]()
            st8["pend_copies"] = None
            norm_all = st8["pend_norm"]()
            st8["pend_norm"] = None

            last_fin = emit_tail(c, xnch_ready, norm_all)

        if debug:
            wdbg = [("DVE", last_fin), ("PE", CNT["PE"]), ("POOL", CNT["POOL"])]
            dma("sync", "DQ", dbg_h["K"], K_fm[:, :], waits=wdbg)
            dma("sync", "DQ", dbg_h["V"], V_ext[:, :], waits=wdbg)
            dma("sync", "DQ", dbg_h["Q"], qfm[:, :], waits=wdbg)
            dma("sync", "DQ", dbg_h["Y"], yreg[:, :], waits=wdbg)

        tot_dq, tot_gq = CNT["DQ"], CNT["GQ"]
        SCHED["sync"].append(lambda e: e.wait_ge(SEMH["DQ"], tot_dq))
        SCHED["gpsimd"].append(lambda e: e.wait_ge(SEMH["GQ"], tot_gq))

        block = es.enter_context(nc.Block())

        @block.sync
        def _(eng):
            for f in SCHED["sync"]:
                f(eng)

        @block.gpsimd
        def _(eng):
            for f in SCHED["gpsimd"]:
                f(eng)

        @block.vector
        def _(eng):
            for f in SCHED["vector"]:
                f(eng)

        @block.scalar
        def _(eng):
            for f in SCHED["scalar"]:
                f(eng)

        @block.tensor
        def _(eng):
            for f in SCHED["tensor"]:
                f(eng)

    return nc


def _pc(WT, n):
    return np.ascontiguousarray(WT.reshape(8, 128, n).transpose(1, 0, 2).reshape(128, 8 * n))


def _host_inputs(inputs):
    x = inputs["L0_attn_x"]
    tg = np.arange(T, dtype=np.float64)
    inv = (1.0 / 10000.0) ** (np.arange(0, DH, 2, dtype=np.float64) / DH)
    f = np.outer(tg, inv)
    mur = np.asarray(inputs["L0_ffn_mu_r"]).reshape(8, 128).T
    muk = np.asarray(inputs["L0_ffn_mu_k"]).reshape(8, 128).T
    shared = {
        "wabqT": _pc(np.asarray(inputs["L0_attn_W_abq"]).T, RQ * CH),
        "wabkvT": _pc(np.asarray(inputs["L0_attn_W_abkv"]).T, (RK + RV) * CH),
        "woT": _pc(np.asarray(inputs["L0_attn_W_o"]).T, D),
        "wrT": _pc(np.asarray(inputs["L0_ffn_W_r"]).T, D),
        "wkT": _pc(np.asarray(inputs["L0_ffn_W_k"]).T, D),
        "wvT": _pc(np.asarray(inputs["L0_ffn_W_v"]).T, D),
        "mufh": np.ascontiguousarray(np.concatenate([mur, muk], axis=1)),
        "rcos": np.ascontiguousarray(np.cos(f).astype(np.float32).reshape(NT, 128, 32)
                                     .transpose(1, 0, 2).reshape(128, NT * 32)),
        "rsin": np.ascontiguousarray(np.sin(f).astype(np.float32).reshape(NT, 128, 32)
                                     .transpose(1, 0, 2).reshape(128, NT * 32)),
    }
    return [dict(shared, x=np.ascontiguousarray(np.asarray(x)[b])) for b in range(8)]


def kernel(**inputs):
    nc = build(debug=DEBUG)
    in_maps = _host_inputs(inputs)
    res = run_bass_kernel_spmd(nc, in_maps, core_ids=list(range(8)))
    return np.stack([np.asarray(r["out"]) for r in res.results], axis=0)


# version 15 (final)
# speedup vs baseline: 1.0275x; 1.0275x over previous
import numpy as np
import ml_dtypes
import contextlib
import concourse.bass as bass
import concourse.mybir as mybir
from concourse.bass_utils import run_bass_kernel_spmd

F32 = mybir.dt.float32
BF16 = mybir.dt.bfloat16
AF = mybir.ActivationFunctionType
ALU = mybir.AluOpType
AX = mybir.AxisListType

T, D = 4096, 1024
NH, DH, CH = 16, 64, 80
RQ, RK, RV = 6, 2, 2
NT = 32
NCH = 4
CHUNK = 1024
SUB = 8
NPAIR = 8
GRP = 8
EXP_BIAS = -20.0
EXP_SCALE = 1.0 / (RQ * RK * float(np.sqrt(DH)))

DEBUG = False
TAGS = {}


def build(debug=False):
    TAGS.clear()
    nc = bass.Bass("TRN2", target_bir_lowering=False, debug=False, dynamic_dma_scratch_size=2048)

    x_h = nc.dram_tensor("x", [T, D], F32, kind="ExternalInput").ap()
    wabqT_h = nc.dram_tensor("wabqT", [128, 8 * RQ * CH], F32, kind="ExternalInput").ap()
    wabkvT_h = nc.dram_tensor("wabkvT", [128, 8 * (RK + RV) * CH], F32, kind="ExternalInput").ap()
    woT_h = nc.dram_tensor("woT", [128, 8 * D], F32, kind="ExternalInput").ap()
    wrT_h = nc.dram_tensor("wrT", [128, 8 * D], F32, kind="ExternalInput").ap()
    wkT_h = nc.dram_tensor("wkT", [128, 8 * D], F32, kind="ExternalInput").ap()
    wvT_h = nc.dram_tensor("wvT", [128, 8 * D], F32, kind="ExternalInput").ap()
    muf_h = nc.dram_tensor("mufh", [128, 16], F32, kind="ExternalInput").ap()
    rcos_h = nc.dram_tensor("rcos", [128, NT * 32], F32, kind="ExternalInput").ap()
    rsin_h = nc.dram_tensor("rsin", [128, NT * 32], F32, kind="ExternalInput").ap()
    out_h = nc.dram_tensor("out", [T, D], F32, kind="ExternalOutput").ap()
    wbq_bf = nc.dram_tensor("wbq_bf", [128, 8 * RQ * CH], BF16, kind="Internal").ap()
    wbkv_bf = nc.dram_tensor("wbkv_bf", [128, 8 * (RK + RV) * CH], BF16, kind="Internal").ap()
    wo_bf = nc.dram_tensor("wo_bf", [128, 8 * D], BF16, kind="Internal").ap()
    wr_bf = nc.dram_tensor("wr_bf", [128, 8 * D], BF16, kind="Internal").ap()
    wk_bf = nc.dram_tensor("wk_bf", [128, 8 * D], BF16, kind="Internal").ap()
    wv_bf = nc.dram_tensor("wv_bf", [128, 8 * D], BF16, kind="Internal").ap()
    xnsp_h = nc.dram_tensor("xnsp", [128, NT * 1024], BF16,
                            kind="ExternalOutput" if debug else "Internal").ap()
    dbg_h = {}
    if debug:
        dbg_h["K"] = nc.dram_tensor("dbg_K", [128, NPAIR * T], BF16, kind="ExternalOutput").ap()
        dbg_h["V"] = nc.dram_tensor("dbg_V", [128, NT * NH * 65], BF16, kind="ExternalOutput").ap()
        dbg_h["Q"] = nc.dram_tensor("dbg_Q", [128, 8 * CHUNK], BF16, kind="ExternalOutput").ap()
        dbg_h["Y"] = nc.dram_tensor("dbg_Y", [128, 8 * CHUNK], BF16, kind="ExternalOutput").ap()

    es = contextlib.ExitStack()
    with es:
        ten = lambda *a: es.enter_context(nc.sbuf_tensor(*a))
        K_fm = ten("K_fm", [128, NPAIR * T], BF16)
        V_ext = ten("V_ext", [128, NT * NH * 65], BF16)
        wstream = ten("wstream", [128, 8 * 1024], BF16)
        yreg = ten("yreg", [128, 8 * 1024], BF16)
        qfm = ten("qfm", [128, 8 * 1024], BF16)
        ptq = ten("ptq", [128, 8 * 1024], BF16)
        xnch = ten("xnch", [128, 8 * 1025], BF16)
        ropec = ten("ropec", [128, NT * 32], BF16)
        ropes = ten("ropes", [128, NT * 32], BF16)
        lnbuf = ten("lnbuf", [128, 1024], F32)
        rrow = ten("rrow", [128, 1024], F32)
        muf = ten("muf", [128, 16], F32)
        consts = ten("consts", [128, 12], F32)
        stats = ten("stats", [128, 12], F32)
        onesb = ten("onesb", [128, 64], F32)
        bbuf = ten("bbuf", [128, 8], BF16)
        ident = ten("ident", [128, 128], BF16)

        psA = es.enter_context(nc.psum_tensor("psA", [128, 2048], F32)).ap()
        psB = es.enter_context(nc.psum_tensor("psB", [128, 2048], F32)).ap()
        psD = psB[:, 1024:2048]
        psD_bf = psB[:, 1024:1536].bitcast(BF16)

        SEMH = {}
        for s in ("DQ", "GQ", "PE", "ACT", "DVE", "POOL"):
            SEMH[s] = es.enter_context(nc.semaphore(s))
        CNT = {k: 0 for k in SEMH}
        SCHED = {k: [] for k in ("sync", "gpsimd", "vector", "scalar", "tensor")}

        def E(engine, fn, waits=(), inc=None, inc_n=1):
            wl = tuple((SEMH[s], v) for (s, v) in waits if v and v > 0)
            tgt = None
            if inc is not None:
                CNT[inc] += inc_n
                tgt = (SEMH[inc], inc_n)
                TAGS.setdefault(inc, {})[CNT[inc]] = getattr(fn, "__name__", "?")

            def run(eng, wl=wl, fn=fn, tgt=tgt):
                for h, v in wl:
                    eng.wait_ge(h, v)
                ins = fn(eng)
                if tgt is not None:
                    ins.then_inc(tgt[0], tgt[1])
            SCHED[engine].append(run)
            return CNT[inc] if inc is not None else None

        def dma(queue, sem, dst, src, waits=()):
            return E(queue, lambda e, d=dst, s=src: e.dma_start(out=d, in_=s),
                     waits, inc=sem, inc_n=16)

        CVALS = [EXP_SCALE, EXP_BIAS, 1e-5, -1.0, -0.5, 0.0, 1.0]
        for i, v in enumerate(CVALS):
            nc.const_aps.register(F32, float(v), consts[:, i:i + 1].opt())

        def Kv(pair, kt):
            return K_fm[:, pair * T + kt * 128: pair * T + kt * 128 + 128]

        def Vv(kt, head):
            off = kt * (NH * 65) + head * 65
            return V_ext[:, off: off + 65]

        def Ptv(par, hh, lo, hi):
            off = par * 1024 + hh * 512
            return ptq[:, off + lo: off + hi]

        def Pt2(par):
            return ptq[:, par * 1024:(par + 1) * 1024].rearrange("p (h w) -> p h w", h=2)

        def psA2(ping):
            return psA[:, ping * 1024:(ping + 1) * 1024].rearrange("p (h w) -> p h w", h=2)

        qp_f32 = ptq[:, 4096: 8192].bitcast(F32)
        ysc_v = qp_f32[:, 0:1024]
        comb = ptq[:, 0:8192].bitcast(F32)
        xc_v = comb[:, 0:1024]
        abqf = comb[:, 0:512]
        cscr_v = comb[:, 1024:2048]
        outst_v = comb[:, 2048:3072]

        def xr_v(cj):
            return ptq[:, cj * 1024:(cj + 1) * 1024]

        def Y_v(pair):
            return yreg[:, pair * 1024:(pair + 1) * 1024]

        def abqT_v(cj):
            return yreg[:, cj * 480:(cj + 1) * 480]

        def abkvT_v(cj):
            return wstream[:, cj * 320:(cj + 1) * 320]

        def xk_v(cj):
            return yreg[:, cj * 1024:(cj + 1) * 1024]

        p1f = yreg[:, 4096: 8192].bitcast(F32)
        xt_v = p1f[:, 0:1024]
        rs1_v = p1f[:, 1024:1152]
        rs2_v = p1f[:, 1152:1280]
        abkvf_v = p1f[:, 1280:1600]
        brot_v = p1f[:, 1600:1856]
        xnb_v = rrow[:, 0:512].bitcast(BF16)
        xnfmt_v = lnbuf[:, 0:512].bitcast(BF16)
        kact = xnch[:, 0:8192]
        xnq = xnch[:, 0:8192].rearrange("p (tile cj t) -> p tile cj t", cj=8, t=128)
        def sktm_pp(i):
            return wstream[:, 7168:8192] if i % 2 == 0 else wstream[:, 6144:7168]
        dmix_v = lnbuf[:, 0:512].bitcast(BF16)
        wsv = wstream.rearrange("p (cj n) -> p cj n", cj=8)
        xnchv = xnch.rearrange("p (cj t) -> p cj t", cj=8)

        def p0_consts(eng):
            for i, v in enumerate(CVALS):
                eng.memset(consts[:, i:i + 1], float(v))
            eng.memset(onesb[:, :], 1.0)
            eng.memset(lnbuf[:, :], 1.0)
            return eng.memset(rrow[:, :], 1.0)
        ms_consts = E("vector", p0_consts, inc="DVE")

        def p0_ident(eng):
            eng.memset(ident[:, :], 1.0)
            return eng.affine_select(out=ident[:, :], in_=ident[:, :],
                                     compare_op=ALU.is_equal, fill=0.0,
                                     base=0, pattern=[[-1, 128]], channel_multiplier=1)
        ms_ident = E("gpsimd", p0_ident, inc="POOL")

        from concourse.hw_specs import get_activation_tables
        set_id = None
        for idx, (nm, funcs) in enumerate(get_activation_tables(nc.m.arch).items()):
            fn_names = {f.name for f in funcs}
            if {"Exp", "Ln"} <= fn_names:
                set_id = idx
                break
        assert set_id is not None

        def p0_act(eng):
            eng.wait_ge(SEMH["DVE"], ms_consts)
            eng.add_instruction(mybir.InstLoadActFuncSet(
                name=nc.get_next_instruction_name(), act_func_set_id=set_id,
                ins=[], outs=[]))
            for _ in range(4):
                eng.nop(cycle_cnt=4000)
            return None
        SCHED["scalar"].append(lambda e: p0_act(e))

        dma("sync", "DQ", muf[:, :], muf_h[:, :])
        ms_mu = CNT["DQ"]

        stage_f = K_fm[:, 0:16384].bitcast(F32)
        stage_b = V_ext[:, 0:8192]
        prep_cast = 0
        prep_out = 0
        for srch, dsth, n in ((rcos_h, None, NT * 32), (rsin_h, None, NT * 32),
                              (wabqT_h, wbq_bf, 8 * RQ * CH),
                              (wabkvT_h, wbkv_bf, 8 * (RK + RV) * CH),
                              (woT_h, wo_bf, 8 * D), (wrT_h, wr_bf, 8 * D),
                              (wkT_h, wk_bf, 8 * D), (wvT_h, wv_bf, 8 * D)):
            wl = [("DVE", prep_cast)] if prep_cast else []
            pld = dma("sync", "DQ", stage_f[:, 0:n], srch[:, :], waits=wl)
            if dsth is None:
                tgt = ropec if srch is rcos_h else ropes
                prep_cast = E("vector",
                              lambda e, tgt=tgt, n=n: e.tensor_copy(tgt[:, 0:n], stage_f[:, 0:n]),
                              waits=[("DQ", pld)], inc="DVE")
            else:
                wl2 = [("DQ", pld)]
                if prep_out:
                    wl2.append(("DQ", prep_out))
                prep_cast = E("vector",
                              lambda e, n=n: e.tensor_copy(stage_b[:, 0:n], stage_f[:, 0:n]),
                              waits=wl2, inc="DVE")
                prep_out = dma("sync", "DQ", dsth[:, :], stage_b[:, 0:n],
                               waits=[("DVE", prep_cast)])
        ms_rope_g = prep_cast
        ms_prep_done = prep_out

        st8 = {
            "xt_consumed": 0,
            "xnsp_done": {},
            "psB_tr_free": 0,
            "psA_small_free": ("DVE", 0),
            "sktm_free_pe": 0,
            "skfree": {0: 0, 1: 0},
            "p1_dve": {}, "p1_pool": {},
            "prev_exp": {},
            "prev_av": {},
            "norm_done_prev": 0,
            "pend_norm": None,
            "pend_copies": None,
            "pend_av": None,
            "yreg_Y_ok": [],
            "y_gate_lo": [],
            "y_gate_hi": [],
            "wstream_free": ("PE", 0),
            "xc_free": 0,
            "outst_dq": 0,
            "cscr_free": 0,
            "out_partial": {},
            "bbuf_saved": 0,
        }

        def _segs(off):
            return ((off, 512), (512, 1024)) if off < 512 else ((off, 1024),)

        p1st = {}

        def emit_p1_tile(ti, abkv_ld, pe_gate):
            w = [("DVE", st8["xt_consumed"]), ("PE", pe_gate)]
            ld = dma("sync", "DQ", xt_v, x_h[ti * 128:(ti + 1) * 128, :], waits=w)

            def ln_stats(eng):
                eng.reduce_sum(stats[:, 0:1], xt_v, axis=AX.X)
                eng.tensor_tensor(qp_f32[:, 0:1024], xt_v, xt_v, ALU.mult)
                return eng.reduce_sum(stats[:, 1:2], qp_f32[:, 0:1024], axis=AX.X)
            msv = E("vector", ln_stats,
                    waits=[("DQ", ld), ("DVE", ms_consts),
                           ("POOL", st8.get("kv_pool", 0))], inc="DVE")

            def ln_stats2(eng):
                eng.tensor_scalar(stats[:, 2:3], stats[:, 0:1], 1.0 / D, None, ALU.mult)
                return eng.tensor_scalar(stats[:, 3:4], stats[:, 1:2], 1.0 / D, None, ALU.mult)
            msv = E("vector", ln_stats2, waits=[("DVE", msv)], inc="DVE")

            msv = E("vector",
                    lambda e: e.scalar_tensor_tensor(stats[:, 5:6], stats[:, 2:3],
                                                     stats[:, 2:3], stats[:, 3:4],
                                                     ALU.mult, ALU.subtract),
                    waits=[("DVE", msv)], inc="DVE")

            msa = E("scalar",
                    lambda e: e.activation(stats[:, 6:7], stats[:, 5:6], AF.Ln,
                                           bias=1e-5, scale=-1.0),
                    waits=[("DVE", msv)], inc="ACT")
            msa = E("scalar",
                    lambda e: e.activation(stats[:, 7:8], stats[:, 6:7], AF.Exp,
                                           bias=0.0, scale=-0.5),
                    waits=[("ACT", msa)], inc="ACT")

            msn = E("vector",
                    lambda e: e.scalar_tensor_tensor(stats[:, 9:10], stats[:, 2:3], -1.0,
                                                     stats[:, 7:8], ALU.mult, ALU.mult),
                    waits=[("ACT", msa)], inc="DVE")
            ms_xnb = E("vector",
                       lambda e: e.tensor_scalar(xnb_v, xt_v, stats[:, 7:8], stats[:, 9:10],
                                                 ALU.mult, ALU.add),
                       waits=[("DVE", msn), ("PE", st8.get("xnb_free_pe", 0))], inc="DVE")
            st8["xt_consumed"] = ms_xnb

            def xpose(eng):
                ins = None
                for cj in range(8):
                    ins = eng.transpose(psD_bf[:, cj * 128:(cj + 1) * 128],
                                        xnb_v[:, cj * 128:(cj + 1) * 128], ident[:, :])
                return ins
            ms_tp = E("tensor", xpose,
                      waits=[("DVE", ms_xnb), ("POOL", ms_ident),
                             ("DVE", st8["psB_tr_free"])], inc="PE")
            st8["xnb_free_pe"] = ms_tp

            wv = [("PE", ms_tp)]
            if ti > 0 and (ti - 1) in st8["xnsp_done"]:
                wv.append(("DQ", st8["xnsp_done"][ti - 1]))
            ms_xnfmt = E("vector", lambda e: e.tensor_copy(xnfmt_v, psD_bf[:, 0:1024]),
                         waits=wv, inc="DVE")
            st8["psB_tr_free"] = ms_xnfmt

            sp = dma("sync", "DQ", xnsp_h[:, ti * 1024:(ti + 1) * 1024],
                     xnfmt_v, waits=[("DVE", ms_xnfmt)])
            st8["xnsp_done"][ti] = sp

            def abkv_mm(eng):
                ins = None
                for cj in range(8):
                    ins = eng.matmul(psD[:, 0:320], xnfmt_v[:, cj * 128:(cj + 1) * 128],
                                     abkvT_v(cj), start=(cj == 0), stop=(cj == 7))
                return ins
            ms_mm = E("tensor", abkv_mm,
                      waits=[("DVE", ms_xnfmt), ("DQ", abkv_ld),
                             st8["psA_small_free"]], inc="PE")
            ms_ev = E("vector", lambda e: e.tensor_copy(abkvf_v, psD[:, 0:320]),
                      waits=[("PE", ms_mm), ("POOL", st8.get("kv_pool", 0))], inc="DVE")
            st8["psA_small_free"] = ("DVE", ms_ev)

            cosv = ropec[:, ti * 32:(ti + 1) * 32][:, None, :].broadcast_to([128, 4, 32])
            sinv = ropes[:, ti * 32:(ti + 1) * 32][:, None, :].broadcast_to([128, 4, 32])
            abkv_j = abkvf_v.rearrange("p (j c) -> p j c", j=4)
            x1v, x2v = abkv_j[:, :, 16:48], abkv_j[:, :, 48:80]
            brot_j = brot_v.rearrange("p (j c) -> p j c", j=4)
            rs1j = rs1_v.rearrange("p (j c) -> p j c", j=4)
            rs2j = rs2_v.rearrange("p (j c) -> p j c", j=4)

            ms_rope = E("vector",
                        lambda e: (e.tensor_tensor(rs1_v, x1v, cosv, ALU.mult),
                                   e.tensor_tensor(rs2_v, x2v, sinv, ALU.mult))[-1],
                        waits=[("DVE", ms_ev), ("DVE", ms_rope_g)], inc="DVE")
            ms_rope = E("vector",
                        lambda e: e.tensor_tensor(brot_j[:, :, 0:32], rs1j, rs2j, ALU.add),
                        waits=[("DVE", ms_rope)], inc="DVE")
            ms_rope = E("vector",
                        lambda e: (e.tensor_tensor(rs1_v, x1v, sinv, ALU.mult),
                                   e.tensor_tensor(rs2_v, x2v, cosv, ALU.mult))[-1],
                        waits=[("DVE", ms_rope)], inc="DVE")
            ms_rope = E("vector",
                        lambda e: e.tensor_tensor(brot_j[:, :, 32:64], rs2j, rs1j, ALU.subtract),
                        waits=[("DVE", ms_rope)], inc="DVE")

            pBv = qp_f32[:, 1024:2048].rearrange("p (h c) -> p h c", h=NH)

            def abj(j):
                return abkvf_v[:, j * 80: j * 80 + 16][:, :, None].broadcast_to([128, NH, DH])

            def brj(j):
                return brot_v[:, j * 64:(j + 1) * 64][:, None, :].broadcast_to([128, NH, DH])

            sktm = sktm_pp(ti)
            sktm3 = sktm[:, 0:1024].rearrange("p (h c) -> p h c", h=NH)

            def kv_s1(eng, sktm3=sktm3):
                eng.tensor_tensor(sktm3, abj(0), brj(0), ALU.mult)
                return eng.tensor_tensor(pBv, abj(1), brj(1), ALU.mult)
            ms_kv = E("gpsimd", kv_s1,
                      waits=[("DVE", ms_rope), ("PE", st8["skfree"][ti % 2])], inc="POOL")
            ms_kv = E("gpsimd",
                      lambda e, sktm=sktm: e.tensor_tensor(
                          sktm[:, 0:1024], sktm[:, 0:1024], qp_f32[:, 1024:2048], ALU.add),
                      waits=[("POOL", ms_kv)], inc="POOL")
            off_v = ti * (NH * 65)
            vdst = V_ext[:, off_v: off_v + NH * 65].rearrange("p (h c) -> p h c", c=65)

            def kv_s3(eng, vdst=vdst):
                eng.tensor_tensor(vdst[:, :, 0:64], abj(2), brj(2), ALU.mult)
                return eng.tensor_tensor(pBv, abj(3), brj(3), ALU.mult)
            ms_kv = E("gpsimd", kv_s3, waits=[("POOL", ms_kv)], inc="POOL")

            def v_fin(eng, vdst=vdst):
                eng.tensor_tensor(vdst[:, :, 0:64], vdst[:, :, 0:64], pBv, ALU.add)
                return eng.memset(vdst[:, :, 64:65], 2.0)
            ms_kv = E("gpsimd", v_fin, waits=[("POOL", ms_kv)], inc="POOL")
            st8["kv_pool"] = ms_kv
            p1st[ti] = ms_kv

        def emit_p1_B(ti):
            sktm = sktm_pp(ti)
            ms_kv = p1st.pop(ti)

            def ktp(eng, sktm=sktm):
                ins = None
                for pr in range(8):
                    ins = eng.transpose(psD_bf[:, pr * 128:(pr + 1) * 128],
                                        sktm[:, pr * 128:(pr + 1) * 128], ident[:, :])
                return ins
            ms_ktp = E("tensor", ktp,
                       waits=[("POOL", ms_kv), ("DVE", st8["psB_tr_free"]),
                              st8["psA_small_free"]], inc="PE")
            st8["sktm_free_pe"] = ms_ktp
            st8["skfree"][ti % 2] = ms_ktp
            Kdst = K_fm.rearrange("p (pr t) -> p pr t", pr=NPAIR)[:, :, ti * 128:(ti + 1) * 128]
            ms_kf = E("vector",
                      lambda e, Kdst=Kdst: e.tensor_copy(
                          Kdst, psD_bf[:, 0:1024].rearrange("p (pr t) -> p pr t", pr=8)),
                      waits=[("PE", ms_ktp)], inc="DVE")
            st8["psB_tr_free"] = ms_kf

        def emit_p1_group(g, yreg_free, deferred=False):
            abkv_ld = dma("sync", "DQ", wstream[:, 0:2560], wbkv_bf[:, :],
                          waits=list(yreg_free) + [("DQ", ms_prep_done),
                                                   ("PE", CNT["PE"])])
            pe_gate = CNT["PE"]

            def fin():
                st8["p1_dve"][g] = CNT["DVE"]
                st8["p1_pool"][g] = CNT["POOL"]
                st8["y_gate_hi"] = [("DVE", CNT["DVE"]), ("POOL", CNT["POOL"])]
                st8["p1_pe_cnt"] = CNT["PE"]
            t0 = g * GRP

            def mkA(ti):
                return lambda: emit_p1_tile(ti, abkv_ld, pe_gate)

            def mkB(ti):
                return lambda: emit_p1_B(ti)

            def mkBA(tb, ta):
                return lambda: (emit_p1_B(tb), emit_p1_tile(ta, abkv_ld, pe_gate))
            tiles = [mkA(t0), mkA(t0 + 1)]
            for j in range(6):
                tiles.append(mkBA(t0 + j, t0 + j + 2))
            tiles += [mkB(t0 + 6), mkB(t0 + 7), fin]
            if not deferred:
                for t in tiles:
                    t()
                return []
            return tiles

        def emit_qbuild(c, yreg_free, ptq_free, qfm_free, xnch_free):
            q0 = c * CHUNK
            w = list(xnch_free) + [("DQ", st8["xnsp_done"][(c + 1) * GRP - 1])]
            ld = dma("sync", "DQ", xnch[:, 0:8192],
                     xnsp_h[:, 8 * c * 1024: (8 * c + 8) * 1024], waits=w)
            if c == 0:
                msz = E("vector", lambda e: e.memset(bbuf[:, :], 0.0),
                        waits=list(xnch_free), inc="DVE")
                xnch_ready = [("DQ", ld), ("DVE", msz)]
            else:
                xnch_ready = [("DQ", ld), ("DVE", st8["bbuf_saved"])]

            pe_at_qb = CNT["PE"]
            abq_ld = dma("sync", "DQ", yreg[:, 0:3840], wbq_bf[:, :],
                         waits=list(yreg_free) + [("DQ", ms_prep_done)])

            ms_q = 0
            ev_d, tp_d, jn_d, pl_d, qv_d = {}, {}, {}, {}, {}
            qc_prev = [0]
            pacc = cscr_v
            pscr = xt_v

            sktm_b = sktm_pp

            def abp(r, which):
                if which == 0:
                    return abqf[:, r * 80: r * 80 + 16][:, :, None].broadcast_to([128, NH, DH])
                return abqf[:, r * 80 + 16: r * 80 + 80][:, None, :].broadcast_to([128, NH, DH])

            def emit_join(sub):
                wj = [("POOL", pl_d[sub])]
                if sub - 2 in tp_d:
                    wj.append(("PE", tp_d[sub - 2]))
                jn_d[sub] = E("vector",
                              lambda e, sub=sub: e.tensor_tensor(
                                  sktm_b(sub)[:, 0:1024], qp_f32[:, 0:1024],
                                  pacc, ALU.add),
                              waits=wj, inc="DVE")

            def emit_qtp(sub):
                sk = sktm_b(sub)

                def qtp(eng, sk=sk):
                    ins = None
                    for pr in range(8):
                        ins = eng.transpose(psD_bf[:, pr * 128:(pr + 1) * 128],
                                            sk[:, pr * 128:(pr + 1) * 128], ident[:, :])
                    return ins
                wt = [("DVE", jn_d[sub]), ("DVE", st8["psB_tr_free"])]
                if sub + 1 in ev_d:
                    wt.append(("ACT", ev_d[sub + 1]))
                ms_qtp = E("tensor", qtp, waits=wt, inc="PE")
                tp_d[sub] = ms_qtp
                st8["sktm_free_pe"] = ms_qtp
                st8["skfree"][sub % 2] = ms_qtp
                Qdst = qfm.rearrange("p (pr t) -> p pr t", pr=NPAIR)[:, :, sub * 128:(sub + 1) * 128]
                wq2 = [("PE", ms_qtp)]
                if sub == 0:
                    wq2 += list(qfm_free)
                msq = E("vector",
                        lambda e, Qdst=Qdst: e.tensor_copy(
                            Qdst, psD_bf[:, 0:1024].rearrange("p (pr t) -> p pr t", pr=8)),
                        waits=wq2, inc="DVE")
                qv_d[sub] = msq
                st8["psB_tr_free"] = msq
                return msq

            for sub in range(SUB):
                def abq_mm(eng, sub=sub):
                    ins = None
                    for cj in range(8):
                        lhsT = xnch[:, sub * 1024 + cj * 128: sub * 1024 + cj * 128 + 128]
                        ins = eng.matmul(psD[:, 0:480], lhsT, abqT_v(cj),
                                         start=(cj == 0), stop=(cj == 7))
                    return ins
                ms_mm = E("tensor", abq_mm,
                          waits=xnch_ready + [("DQ", abq_ld), st8["psA_small_free"],
                                              ("DVE", st8["psB_tr_free"])],
                          inc="PE")
                wv = [("PE", ms_mm)]
                if sub == 0:
                    wv += list(ptq_free)
                else:
                    wv += [("DVE", qc_prev[0]), ("POOL", pl_d[sub - 1])]
                ms_ev = E("scalar",
                          lambda e: e.activation(abqf[:, 0:480], psD[:, 0:480],
                                                 AF.Identity, bias=0.0, scale=1.0),
                          waits=wv, inc="ACT")
                st8["psA_small_free"] = ("ACT", ms_ev)
                ev_d[sub] = ms_ev

                if sub >= 1:
                    emit_join(sub - 1)

                def qcon_pool(eng, sub=sub):
                    pacc3 = pacc.rearrange("p (h c) -> p h c", h=NH)
                    pscr3 = pscr.rearrange("p (h c) -> p h c", h=NH)
                    eng.tensor_tensor(pacc3, abp(4, 0), abp(4, 1), ALU.mult)
                    eng.tensor_tensor(pscr3, abp(5, 0), abp(5, 1), ALU.mult)
                    return eng.tensor_tensor(pacc, pacc, pscr, ALU.add)
                wp = [("ACT", ms_ev)]
                if sub == 0:
                    wp += [("PE", pe_at_qb), ("DVE", st8["cscr_free"]),
                           ("DVE", st8["xt_consumed"]), ("DQ", st8["outst_dq"])]
                else:
                    wp.append(("DVE", jn_d[sub - 1]))
                pl_d[sub] = E("gpsimd", qcon_pool, waits=wp, inc="POOL")

                def qcon(eng, sub=sub):
                    qacc = qp_f32[:, 0:1024].rearrange("p (h c) -> p h c", h=NH)
                    qscr = qp_f32[:, 1024:2048].rearrange("p (h c) -> p h c", h=NH)
                    ins = eng.tensor_tensor(qacc, abp(0, 0), abp(0, 1), ALU.mult)
                    for r in (1, 2, 3):
                        eng.tensor_tensor(qscr, abp(r, 0), abp(r, 1), ALU.mult)
                        ins = eng.tensor_tensor(qacc, qacc, qscr, ALU.add)
                    return ins
                wq = [("ACT", ms_ev)]
                if sub == 0:
                    wq.append(("PE", pe_at_qb))
                    wq.append(("POOL", CNT["POOL"] - 1))
                qc_prev[0] = E("vector", qcon, waits=wq, inc="DVE")

                if sub >= 1:
                    ms_q = emit_qtp(sub - 1)
            emit_join(SUB - 1)
            ms_q = emit_qtp(SUB - 1)
            return ms_q, xnch_ready

        def make_norm(s, pair, av_cell):
            half = s % 2
            cell_nc = [0]

            def docopies():
                av_last = av_cell[0]
                def norm_copies(eng):
                    eng.tensor_copy(ysc_v[0:64, 0:512], psB[0:64, 0:512])
                    eng.tensor_copy(ysc_v[64:128, 0:512], psB[0:64, 512:1024])
                    eng.tensor_copy(lnbuf[0:1, 0:512], psB[64:65, 0:512])
                    return eng.tensor_copy(lnbuf[32:33, 0:512], psB[64:65, 512:1024])
                ms_nc = E("vector", norm_copies,
                          waits=[("PE", av_last), ("DQ", st8["outst_dq"])], inc="DVE")
                cell_nc[0] = ms_nc
                st8["psB_av_free"] = ms_nc

            def donorm():
                ms_nc = cell_nc[0]
                ms_le = E("scalar",
                          lambda e: e.activation(rrow[0:33, 0:512], lnbuf[0:33, 0:512],
                                                 AF.Ln, bias=0.0, scale=1.0),
                          waits=[("DVE", ms_nc)], inc="ACT")
                ms_le = E("scalar",
                          lambda e: e.activation(rrow[0:33, 0:512], rrow[0:33, 0:512],
                                                 AF.Exp, bias=0.0, scale=-1.0),
                          waits=[("ACT", ms_le)], inc="ACT")

                def norm_rs(eng):
                    ins = None
                    for hh, rbase in ((0, 0), (1, 32)):
                        ins = eng.matmul(psD[hh * 64:(hh + 1) * 64, 512:1024],
                                         onesb[rbase:rbase + 1, :],
                                         rrow[rbase:rbase + 1, 0:512],
                                         start=True, stop=True,
                                         tile_position=(rbase, hh * 64))
                    return ins
                ms_rs = E("tensor", norm_rs,
                          waits=[("ACT", ms_le),
                                 ("DVE", st8.get("psD_norm_free", 0))], inc="PE")

                gate = st8["y_gate_lo"] if pair < 4 else st8["y_gate_hi"]
                ms_nm = E("vector",
                          lambda e: e.tensor_tensor(
                              Y_v(pair)[:, half * 512:(half + 1) * 512],
                              ysc_v[:, 0:512], psD[:, 512:1024], ALU.mult),
                          waits=[("PE", ms_rs)] + list(gate), inc="DVE")
                st8["norm_done_prev"] = ms_nm
                st8["psD_norm_free"] = ms_nm
                return ms_nm
            return docopies, donorm

        def emit_subchunk(s, ms_q, p1_tiles):
            q0 = 512 * s
            NKT = 4 * (s + 1)
            qv = qfm.rearrange("p (pr t) -> p pr t", pr=NPAIR)
            half = s % 2

            for pair in range(NPAIR):
                hA, hB = 2 * pair, 2 * pair + 1
                exp_ms, av_ms, mask_ms = {}, {}, {}

                def emit_S(kt):
                    off = max(0, kt * 128 - q0)
                    ping = kt % 2

                    def smm(eng, kt=kt, off=off, pair=pair, ping=ping):
                        ins = None
                        for hh, base in ((0, 0), (1, 64)):
                            lhsT = Kv(pair, kt)[base:base + 64, :]
                            ins = eng.matmul(
                                psA[:, ping * 1024 + hh * 512 + off: ping * 1024 + hh * 512 + 512],
                                lhsT,
                                qv[base:base + 64, pair, half * 512 + off: half * 512 + 512],
                                start=True, stop=True, tile_position=(base, 0))
                        return ins
                    w = [("DVE", ms_q), ("DVE", st8["p1_dve"][kt // 8]),
                         ("POOL", st8["p1_pool"][kt // 8])]
                    if kt >= 2:
                        w.append(("ACT", exp_ms[kt - 2]))
                    else:
                        pk = st8["prev_exp"].get(kt)
                        if pk:
                            w.append(("ACT", pk))
                    if kt == 0:
                        w.append(st8["psA_small_free"])
                    return E("tensor", smm, waits=w, inc="PE")

                def emit_exp(kt):
                    off = max(0, kt * 128 - q0)
                    ping = kt % 2
                    par = kt % 4

                    def expf(eng, off=off, ping=ping, par=par):
                        return eng.activation(Pt2(par)[:, :, off:512],
                                              psA2(ping)[:, :, off:512],
                                              AF.Exp, bias=EXP_BIAS, scale=EXP_SCALE)
                    we = [("PE", s_ms[kt])]
                    if kt >= 4:
                        we.append(("PE", av_ms[kt - 4]))
                    else:
                        pk = st8["prev_av"].get(kt)
                        if pk:
                            we.append(("PE", pk))
                    exp_ms[kt] = E("scalar", expf, waits=we, inc="ACT")
                    if kt * 128 >= q0:
                        w_cols = 512 - off

                        def maskf(eng, off=off, par=par, w_cols=w_cols):
                            return eng.affine_select(
                                out=Pt2(par)[:, :, off:512], in_=Pt2(par)[:, :, off:512],
                                compare_op=ALU.is_ge, fill=0.0,
                                base=0, pattern=[[0, 2], [1, w_cols]], channel_multiplier=-1)
                        mask_ms[kt] = E("gpsimd", maskf, waits=[("ACT", exp_ms[kt])], inc="POOL")

                def emit_AV(kt, exp_ms=exp_ms, av_ms=av_ms, mask_ms=mask_ms, hA=hA, hB=hB, NKT=NKT, q0=q0):
                    off = max(0, kt * 128 - q0)
                    ping = kt % 2

                    par = kt % 4

                    def avf(eng, kt=kt, off=off, par=par, NKT=NKT, hA=hA, hB=hB):
                        ins = None
                        for hh, head in ((0, hA), (1, hB)):
                            ins = eng.matmul(psB[0:65, hh * 512 + off: hh * 512 + 512],
                                             Vv(kt, head), Ptv(par, hh, off, 512),
                                             start=(kt == 0), stop=(kt == NKT - 1))
                        return ins
                    wa = [("ACT", exp_ms[kt])]
                    if kt in mask_ms:
                        wa.append(("POOL", mask_ms[kt]))
                    if kt == 0:
                        wa.append(("DVE", st8.get("psB_av_free", 0)))
                    av_ms[kt] = E("tensor", avf, waits=wa, inc="PE")

                s_ms = {}
                norm_kt = 3 if NKT <= 4 else (5 if NKT <= 8 else 7)
                for kt in range(NKT):
                    s_ms[kt] = emit_S(kt)
                    emit_exp(kt)
                    if kt == 0 and st8["pend_av"] is not None:
                        st8["pend_av"]()
                        st8["pend_av"] = None
                    if kt == 1 and st8["pend_copies"] is not None:
                        st8["pend_copies"]()
                        st8["pend_copies"] = None
                    if kt == norm_kt and st8["pend_norm"] is not None:
                        st8["pend_norm"]()
                        st8["pend_norm"] = None
                    if kt >= 1:
                        emit_AV(kt - 1)
                av_cell = [None]

                def pend_av(emit_AV=emit_AV, av_ms=av_ms, av_cell=av_cell, NKT=NKT):
                    emit_AV(NKT - 1)
                    av_cell[0] = av_ms[NKT - 1]
                    st8["prev_av"] = {(kk - (NKT - 4)): av_ms[kk]
                                      for kk in range(max(0, NKT - 4), NKT)}
                st8["prev_exp"] = {0: exp_ms[NKT - 2], 1: exp_ms[NKT - 1]}
                st8["prev_av"] = {(kk - (NKT - 4)): av_ms[kk]
                                  for kk in range(max(0, NKT - 4), NKT - 1)}
                st8["pend_av"] = pend_av
                st8["pend_copies"], st8["pend_norm"] = make_norm(s, pair, av_cell)

                for _ in range(2):
                    if p1_tiles:
                        p1_tiles.pop(0)()
                        if p1_tiles and not callable(p1_tiles[0]):
                            p1_tiles.pop(0)()

        def emit_tail(c, xnch_ready, norm_all):
            q0 = c * CHUNK
            yv = yreg.rearrange("p (pr t) -> p pr t", pr=NPAIR)

            wo_ld0 = dma("sync", "DQ", wstream[:, 0:7168], wo_bf[:, 0:7168],
                         waits=[("PE", st8["p1_pe_cnt"]), st8["wstream_free"]])
            wo_ld = dma("sync", "DQ", wstream[:, 7168:8192], wo_bf[:, 7168:8192],
                        waits=[("PE", CNT["PE"])])
            wo_evac = {}
            for sub in range(SUB):
                wx = [("DVE", st8["xc_free"])] if st8["xc_free"] else [("DVE", norm_all)]
                xl = dma("sync", "DQ", xc_v, x_h[q0 + sub * 128: q0 + sub * 128 + 128, :], waits=wx)
                ms_p1 = 0
                for hf in range(2):
                    pc = sub * 2 + hf

                    def wo_mm(eng, sub=sub, hf=hf):
                        ins = None
                        for pr in range(8):
                            lhsT = yv[:, pr, sub * 128:(sub + 1) * 128]
                            ins = eng.matmul(psD[:, hf * 512:(hf + 1) * 512], lhsT,
                                             wsv[:, pr, hf * 512:(hf + 1) * 512],
                                             start=(pr == 0), stop=(pr == 7))
                        return ins
                    w = [("DQ", wo_ld), ("DVE", norm_all)]
                    if pc == 0:
                        w.append(st8["psA_small_free"])
                        w.append(("DVE", st8["psB_tr_free"]))
                    if pc >= 2:
                        w.append(("DVE", wo_evac[pc - 2]))
                    ms_wo = E("tensor", wo_mm, waits=w, inc="PE")

                    def part1(eng, hf=hf):
                        return eng.tensor_tensor(outst_v[:, hf * 512:(hf + 1) * 512],
                                                 psD[:, hf * 512:(hf + 1) * 512],
                                                 xc_v[:, hf * 512:(hf + 1) * 512], ALU.add)
                    wp = [("PE", ms_wo), ("DQ", xl)]
                    if st8["outst_dq"]:
                        wp.append(("DQ", st8["outst_dq"]))
                    ms_p1 = E("vector", part1, waits=wp, inc="DVE")
                    wo_evac[pc] = ms_p1
                st8["xc_free"] = ms_p1
                stq = dma("sync", "DQ", out_h[q0 + sub * 128: q0 + sub * 128 + 128, :],
                          outst_v, waits=[("DVE", ms_p1)])
                st8["outst_dq"] = stq
                st8["out_partial"][(c, sub)] = stq
            pe_after_wo = CNT["PE"]
            st8["wstream_free"] = ("PE", pe_after_wo)

            wr_ld = dma("sync", "DQ", wstream[:, :], wr_bf[:, :],
                        waits=[st8["wstream_free"]])

            last_st = st8["out_partial"][(c, SUB - 1)]
            dmix3 = dmix_v.rearrange("p (tile t) -> p tile t", t=128)
            dmx2_3 = rrow[:, 512:1024].bitcast(BF16).rearrange("p (tile t) -> p tile t", t=128)
            mix_w = [("DQ", last_st), ("DQ", ms_mu),
                     ("PE", pe_after_wo), ("PE", st8["sktm_free_pe"])] + list(xnch_ready)

            def emit_mix_d(cj, d3, wts):
                def mix_d(eng, cj=cj, d3=d3):
                    eng.tensor_tensor(d3[:, :, 1:128], xnq[:, :, cj, 1:128],
                                      xnq[:, :, cj, 0:127], ALU.subtract)
                    eng.tensor_tensor(d3[:, 1:8, 0:1], xnq[:, 1:8, cj, 0:1],
                                      xnq[:, 0:7, cj, 127:128], ALU.subtract)
                    return eng.tensor_tensor(d3[:, 0:1, 0:1], xnq[:, 0:1, cj, 0:1],
                                             bbuf[:, cj:cj + 1], ALU.subtract)
                return E("vector", mix_d, waits=wts, inc="DVE")

            def emit_mix_s(cj, d3, dst, mcol, ms_d):
                def mix_s(eng, cj=cj, d3=d3, dst=dst, mcol=mcol):
                    dst3 = dst[:, cj * 1024:(cj + 1) * 1024].rearrange(
                        "p (tile t) -> p tile t", t=128)
                    eng.scalar_tensor_tensor(dst3[:, :, 1:128], d3[:, :, 1:128], mcol,
                                             xnq[:, :, cj, 0:127], ALU.mult, ALU.add)
                    eng.scalar_tensor_tensor(dst3[:, 1:8, 0:1], d3[:, 1:8, 0:1], mcol,
                                             xnq[:, 0:7, cj, 127:128], ALU.mult, ALU.add)
                    return eng.scalar_tensor_tensor(dst3[:, 0:1, 0:1], d3[:, 0:1, 0:1], mcol,
                                                    bbuf[:, cj:cj + 1], ALU.mult, ALU.add)
                return E("vector", mix_s, waits=[("DVE", ms_d)], inc="DVE")

            ms_mix_r = 0
            for cj in range(8):
                wmd = list(mix_w) if cj == 0 else [("DVE", ms_mix_r)]
                ms_d = emit_mix_d(cj, dmix3, wmd)
                ms_mix_r = emit_mix_s(cj, dmix3, ptq, muf[:, cj:cj + 1], ms_d)
            ms_mix = ms_mix_r
            for cj in range(8):
                ms_d = emit_mix_d(cj, dmx2_3, [("DVE", ms_mix)])
                ms_mix = emit_mix_s(cj, dmx2_3, yreg, muf[:, 8 + cj:9 + cj], ms_d)
            ms_bb = E("vector",
                      lambda e: e.tensor_copy(bbuf[:, :], xnq[:, 7, :, 127]),
                      waits=[("DVE", ms_mix)], inc="DVE")
            st8["bbuf_saved"] = ms_bb

            r_done = {}
            for sub in range(SUB):
                for hf in range(2):
                    pc = sub * 2 + hf

                    def r_mm(eng, sub=sub, hf=hf):
                        ins = None
                        for cj in range(8):
                            lhsT = ptq[:, cj * 1024 + sub * 128: cj * 1024 + sub * 128 + 128]
                            ins = eng.matmul(psD[:, hf * 512:(hf + 1) * 512], lhsT,
                                             wsv[:, cj, hf * 512:(hf + 1) * 512],
                                             start=(cj == 0), stop=(cj == 7))
                        return ins
                    w = [("DQ", wr_ld), ("DVE", ms_mix_r)]
                    if pc >= 2:
                        w.append(("ACT", r_done[pc - 2]))
                    ms_rmm = E("tensor", r_mm, waits=w, inc="PE")

                    lnh = lnbuf[:, hf * 512:(hf + 1) * 512]
                    ms_s1 = E("scalar",
                              lambda e, hf=hf, lnh=lnh: e.activation(
                                  lnh, psD[:, hf * 512:(hf + 1) * 512],
                                  AF.Exp, bias=0.0, scale=-1.0),
                              waits=[("PE", ms_rmm)], inc="ACT")
                    ms_s2 = E("scalar",
                              lambda e, lnh=lnh: e.activation(lnh, lnh, AF.Ln,
                                                              bias=1.0, scale=1.0),
                              waits=[("ACT", ms_s1)], inc="ACT")
                    r_done[pc] = E("scalar",
                                   lambda e, sub=sub, hf=hf, lnh=lnh: e.activation(
                                       qfm[:, sub * 1024 + hf * 512: sub * 1024 + hf * 512 + 512],
                                       lnh, AF.Exp, bias=0.0, scale=-1.0),
                                   waits=[("ACT", ms_s2)], inc="ACT")
            ms_r_all = r_done[2 * SUB - 1]
            pe_after_r = CNT["PE"]
            st8["wstream_free"] = ("PE", pe_after_r)

            wk_ld = dma("sync", "DQ", wstream[:, :], wk_bf[:, :],
                        waits=[st8["wstream_free"]])
            k_done = {}
            for dblk in range(8):
                for hf in range(2):
                    pc = dblk * 2 + hf

                    def k_mm(eng, dblk=dblk, hf=hf):
                        ins = None
                        for cj in range(8):
                            lhsT = wsv[:, cj, dblk * 128:(dblk + 1) * 128]
                            ins = eng.matmul(psD[:, hf * 512:(hf + 1) * 512], lhsT,
                                             yreg[:, cj * 1024 + hf * 512: cj * 1024 + hf * 512 + 512],
                                             start=(cj == 0), stop=(cj == 7))
                        return ins
                    w = [("DQ", wk_ld), ("DVE", ms_mix)]
                    if pc >= 2:
                        w.append(("DVE", k_done[pc - 2]))
                    ms_kmm = E("tensor", k_mm, waits=w, inc="PE")

                    def krelu(eng, hf=hf):
                        return eng.activation(cscr_v[:, hf * 512:(hf + 1) * 512],
                                              psD[:, hf * 512:(hf + 1) * 512],
                                              AF.Relu, bias=0.0, scale=1.0)
                    wka = [("PE", ms_kmm), ("PE", pe_after_r)]
                    if st8["cscr_free"]:
                        wka.append(("DVE", st8["cscr_free"]))
                    ms_kr = E("scalar", krelu, waits=wka, inc="ACT")

                    def ksq(eng, dblk=dblk, hf=hf):
                        cs = cscr_v[:, hf * 512:(hf + 1) * 512]
                        return eng.tensor_tensor(
                            kact[:, dblk * 1024 + hf * 512: dblk * 1024 + hf * 512 + 512],
                            cs, cs, ALU.mult)
                    k_done[pc] = E("vector", ksq, waits=[("ACT", ms_kr)], inc="DVE")
                    st8["cscr_free"] = k_done[pc]
            ms_k_all = k_done[15]
            pe_after_k = CNT["PE"]
            st8["wstream_free"] = ("PE", pe_after_k)

            wv_ld = dma("sync", "DQ", wstream[:, :], wv_bf[:, :],
                        waits=[st8["wstream_free"]])
            fin_done = {}
            for sub in range(SUB):
                ol = dma("sync", "DQ", xc_v, out_h[q0 + sub * 128: q0 + sub * 128 + 128, :],
                         waits=[("DQ", st8["out_partial"][(c, sub)]),
                                ("DVE", st8["xc_free"]), ("PE", pe_after_r)])
                ms_f = 0
                for hf in range(2):
                    pc = sub * 2 + hf

                    def v_mm(eng, sub=sub, hf=hf):
                        ins = None
                        for dblk in range(8):
                            lhsT = kact[:, dblk * 1024 + sub * 128: dblk * 1024 + sub * 128 + 128]
                            ins = eng.matmul(psD[:, hf * 512:(hf + 1) * 512], lhsT,
                                             wsv[:, dblk, hf * 512:(hf + 1) * 512],
                                             start=(dblk == 0), stop=(dblk == 7))
                        return ins
                    w = [("DQ", wv_ld), ("DVE", ms_k_all), ("ACT", ms_r_all)]
                    if pc >= 2:
                        w.append(("DVE", fin_done[pc - 2]))
                    ms_vmm = E("tensor", v_mm, waits=w, inc="PE")

                    def fin(eng, sub=sub, hf=hf):
                        cs = cscr_v[:, hf * 512:(hf + 1) * 512]
                        eng.tensor_tensor(cs, qfm[:, sub * 1024 + hf * 512: sub * 1024 + hf * 512 + 512],
                                          psD[:, hf * 512:(hf + 1) * 512], ALU.mult)
                        return eng.tensor_tensor(outst_v[:, hf * 512:(hf + 1) * 512],
                                                 cs, xc_v[:, hf * 512:(hf + 1) * 512], ALU.add)
                    wf = [("PE", ms_vmm), ("DQ", ol), ("DVE", st8["cscr_free"]),
                          ("DQ", st8["outst_dq"])]
                    ms_f = E("vector", fin, waits=wf, inc="DVE")
                    fin_done[pc] = ms_f
                    st8["cscr_free"] = ms_f
                st8["xc_free"] = ms_f
                stq = dma("sync", "DQ", out_h[q0 + sub * 128: q0 + sub * 128 + 128, :],
                          outst_v, waits=[("DVE", ms_f)])
                st8["outst_dq"] = stq
            st8["wstream_free"] = ("PE", CNT["PE"])
            st8["psA_small_free"] = ("DVE", fin_done[2 * SUB - 1])
            st8["psD_norm_free"] = fin_done[2 * SUB - 1]
            st8["psB_tr_free"] = ms_k_all
            return fin_done[SUB - 1]

        emit_p1_group(0, [("PE", 0)])
        last_fin = 0
        for c in range(NCH):
            if c == 0:
                yreg_free = [("POOL", 0)]
                ptq_free = [("DVE", 0)]
                qfm_free = [("DVE", 0)]
                xnch_free = [("DVE", 0)]
            else:
                yreg_free = [("DVE", last_fin), ("PE", CNT["PE"])]
                ptq_free = [("DVE", last_fin)]
                qfm_free = [("DVE", last_fin)]
                xnch_free = [("PE", CNT["PE"])]

            ms_q, xnch_ready = emit_qbuild(c, yreg_free, ptq_free, qfm_free, xnch_free)
            st8["y_gate_lo"] = [("PE", CNT["PE"])]

            if c + 1 <= 3:
                p1_tiles = emit_p1_group(c + 1, [("PE", CNT["PE"])], deferred=True)
            else:
                p1_tiles = []
                st8["p1_pe_cnt"] = CNT["PE"]

            emit_subchunk(2 * c, ms_q, p1_tiles)
            while p1_tiles:
                p1_tiles.pop(0)()
            emit_subchunk(2 * c + 1, ms_q, [])
            st8["pend_av"]()
            st8["pend_av"] = None
            st8["pend_copies"]()
            st8["pend_copies"] = None
            norm_all = st8["pend_norm"]()
            st8["pend_norm"] = None

            last_fin = emit_tail(c, xnch_ready, norm_all)

        if debug:
            wdbg = [("DVE", last_fin), ("PE", CNT["PE"]), ("POOL", CNT["POOL"])]
            dma("sync", "DQ", dbg_h["K"], K_fm[:, :], waits=wdbg)
            dma("sync", "DQ", dbg_h["V"], V_ext[:, :], waits=wdbg)
            dma("sync", "DQ", dbg_h["Q"], qfm[:, :], waits=wdbg)
            dma("sync", "DQ", dbg_h["Y"], yreg[:, :], waits=wdbg)

        tot_dq, tot_gq = CNT["DQ"], CNT["GQ"]
        SCHED["sync"].append(lambda e: e.wait_ge(SEMH["DQ"], tot_dq))
        SCHED["gpsimd"].append(lambda e: e.wait_ge(SEMH["GQ"], tot_gq))

        block = es.enter_context(nc.Block())

        @block.sync
        def _(eng):
            for f in SCHED["sync"]:
                f(eng)

        @block.gpsimd
        def _(eng):
            for f in SCHED["gpsimd"]:
                f(eng)

        @block.vector
        def _(eng):
            for f in SCHED["vector"]:
                f(eng)

        @block.scalar
        def _(eng):
            for f in SCHED["scalar"]:
                f(eng)

        @block.tensor
        def _(eng):
            for f in SCHED["tensor"]:
                f(eng)

    return nc


def _pc(WT, n):
    return np.ascontiguousarray(WT.reshape(8, 128, n).transpose(1, 0, 2).reshape(128, 8 * n))


def _host_inputs(inputs):
    x = inputs["L0_attn_x"]
    tg = np.arange(T, dtype=np.float64)
    inv = (1.0 / 10000.0) ** (np.arange(0, DH, 2, dtype=np.float64) / DH)
    f = np.outer(tg, inv)
    mur = np.asarray(inputs["L0_ffn_mu_r"]).reshape(8, 128).T
    muk = np.asarray(inputs["L0_ffn_mu_k"]).reshape(8, 128).T
    shared = {
        "wabqT": _pc(np.asarray(inputs["L0_attn_W_abq"]).T, RQ * CH),
        "wabkvT": _pc(np.asarray(inputs["L0_attn_W_abkv"]).T, (RK + RV) * CH),
        "woT": _pc(np.asarray(inputs["L0_attn_W_o"]).T, D),
        "wrT": _pc(np.asarray(inputs["L0_ffn_W_r"]).T, D),
        "wkT": _pc(np.asarray(inputs["L0_ffn_W_k"]).T, D),
        "wvT": _pc(np.asarray(inputs["L0_ffn_W_v"]).T, D),
        "mufh": np.ascontiguousarray(np.concatenate([mur, muk], axis=1)),
        "rcos": np.ascontiguousarray(np.cos(f).astype(np.float32).reshape(NT, 128, 32)
                                     .transpose(1, 0, 2).reshape(128, NT * 32)),
        "rsin": np.ascontiguousarray(np.sin(f).astype(np.float32).reshape(NT, 128, 32)
                                     .transpose(1, 0, 2).reshape(128, NT * 32)),
    }
    return [dict(shared, x=np.ascontiguousarray(np.asarray(x)[b])) for b in range(8)]


def kernel(**inputs):
    nc = build(debug=DEBUG)
    in_maps = _host_inputs(inputs)
    res = run_bass_kernel_spmd(nc, in_maps, core_ids=list(range(8)))
    return np.stack([np.asarray(r["out"]) for r in res.results], axis=0)
